# version 03
# baseline (speedup 1.0000x reference)
import numpy as np
import ml_dtypes
from contextlib import ExitStack

import concourse.bass as bass
import concourse.mybir as mybir
from concourse.bass_utils import run_bass_kernel_spmd

F32 = mybir.dt.float32
BF16 = mybir.dt.bfloat16
F16 = mybir.dt.float16
I32 = mybir.dt.int32
AF = mybir.ActivationFunctionType
ALU = mybir.AluOpType
BF = ml_dtypes.bfloat16

H, D, BS, DM = 32, 64, 256, 2048
T = 4096
NBLK = T // BS
NCH = DM // 128
TC = 512
NC0 = T // TC
TT = 512
NT1 = T // TT
GU_MC = 88
GU_W = 256
DN_MC = 16
DN_W = 704
ID_ = 5632
EPS_ATTN = 1e-5
EPS_FFN = 1e-6
EPS_LN = 1e-5


def _decay_tables():
    h = np.arange(H, dtype=np.float64)
    slope = (0.5 ** (0.25 * (h + 1.0))) * (1.0 - 0.0 / (31 + 1e-5) + 1e-5)
    i = np.arange(BS, dtype=np.float64)
    qd = np.exp(-slope[:, None] * (i[None, :] + 1.0))
    kd = np.exp(-slope[:, None] * (BS - (i[None, :] + 1.0)))
    bd = np.exp(-slope * BS)
    dif = i[:, None] - i[None, :]
    dd = np.where(dif >= 0, np.exp(-slope[:, None, None] * dif[None]), 0.0)
    return slope, qd, kd, bd, dd


def _build_consts(mask_c):
    slope, qd, kd, bd, dd = _decay_tables()
    ddT = np.zeros((128, H * 2 * 256), np.float32)
    for pr in range(16):
        for hh in range(2):
            hd = 2 * pr + hh
            for jh in range(2):
                c0 = ((pr * 2 + hh) * 2 + jh) * 256
                ddT[:, c0:c0 + 256] = dd[hd, :, jh * 128:(jh + 1) * 128].T
    qdm = np.zeros((128, 16 * 256), np.float32)
    for pr in range(16):
        qdm[0:64, pr * 256:(pr + 1) * 256] = qd[2 * pr][None, :]
        qdm[64:128, pr * 256:(pr + 1) * 256] = qd[2 * pr + 1][None, :]
    kdm = np.zeros((128, 2 * 2048), np.float32)
    for jh in range(2):
        for pr in range(16):
            for hh in range(2):
                col = jh * 2048 + pr * 128 + hh * 64
                kdm[:, col:col + 64] = kd[2 * pr + hh, jh * 128:(jh + 1) * 128][:, None]
    bdm = np.zeros((128, 1024), np.float32)
    for hg in range(2):
        for hc in range(16):
            bdm[hg * 64:(hg + 1) * 64, hc * 64:(hc + 1) * 64] = bd[2 * hc + hg]
    tfull = np.arange(512, dtype=np.float64)
    qdf = np.exp(-slope[:, None] * (tfull[None, :] + 1.0))
    qd512 = np.zeros((128, 16 * 512), np.float32)
    for pr in range(16):
        qd512[0:64, pr * 512:pr * 512 + 512] = mask_c * qdf[2 * pr][None, :]
        qd512[64:128, pr * 512:pr * 512 + 512] = mask_c * qdf[2 * pr + 1][None, :]
    return dict(ddT=ddT.astype(BF), qdm=qdm.astype(BF), kdm=kdm.astype(BF),
                bdm=bdm, qd512=qd512.astype(BF))


ENGS = ("sync", "scalar", "vector", "tensor", "gpsimd")


class Plan:
    def __init__(self):
        self.acts = {e: [] for e in ENGS}
        self.counts = {}

    def do(self, eng, fn, waits=(), incs=()):
        ws = tuple((s, v) for w in waits if w is not None for (s, v) in (w,) if v > 0)
        self.acts[eng].append((fn, ws, tuple(incs)))
        for s, d in incs:
            self.counts[s] = self.counts.get(s, 0) + d

    def tok(self, sem):
        return (sem, self.counts.get(sem, 0))

    def emit(self, eng_name, eng, sems):
        hwm = {}
        for fn, waits, incs in self.acts[eng_name]:
            for s, v in waits:
                if hwm.get(s, 0) < v:
                    eng.wait_ge(sems[s], v)
                    hwm[s] = v
            inst = fn(eng)
            for s, d in incs:
                inst = inst.then_inc(sems[s], d)


class Arena:
    def __init__(self, tile, total_cols):
        self.t = tile
        self.total = total_cols
        self.off = 0
        self.regions = {}

    def alloc(self, name, cols):
        cols = (cols + 1) & ~1
        assert self.off + cols <= self.total, f"arena overflow at {name}"
        self.regions[name] = (self.off, cols)
        self.off += cols
        return self.regions[name]

    def bf(self, name, cols=None, off=0):
        o, c = self.regions[name]
        if cols is None:
            cols = c - off
        assert off + cols <= c, f"bf {name} oob"
        return self.t[:, o + off:o + off + cols]

    def bfr(self, name, rows, cols, off=0):
        o, c = self.regions[name]
        assert off + cols <= c
        return self.t[0:rows, o + off:o + off + cols]

    def f32(self, name, cols=None, off=0):
        o, c = self.regions[name]
        if cols is None:
            cols = c // 2 - off
        assert 2 * (off + cols) <= c, f"f32 {name} oob"
        return self.t[:, o + 2 * off:o + 2 * (off + cols)].bitcast(F32)

    def i32(self, name, cols=None, off=0):
        o, c = self.regions[name]
        if cols is None:
            cols = c // 2 - off
        assert 2 * (off + cols) <= c, f"i32 {name} oob"
        return self.t[:, o + 2 * off:o + 2 * (off + cols)].bitcast(I32)


def build_nc(debug=(), stop_after=None, dq_on=True):
    nc = bass.Bass("TRN2", target_bir_lowering=False)

    x_fm = nc.dram_tensor("x_fm", [128, NCH, T], F32, kind="ExternalInput")
    qkvT = nc.dram_tensor("qkvT", [DM, 3 * DM], F32, kind="ExternalInput")
    gateT = nc.dram_tensor("gateT", [DM, DM], F32, kind="ExternalInput")
    outT = nc.dram_tensor("outT", [DM, DM], F32, kind="ExternalInput")
    normw = nc.dram_tensor("normw", [128, NCH], F32, kind="ExternalInput")
    gup = nc.dram_tensor("gup", [2 * ID_, GU_W], I32, kind="ExternalInput")
    gusc = nc.dram_tensor("gusc", [2 * ID_, 16], F16, kind="ExternalInput")
    dnp = nc.dram_tensor("dnp", [DM, DN_W], I32, kind="ExternalInput")
    dnsc = nc.dram_tensor("dnsc", [DM, 44], F16, kind="ExternalInput")
    ddT_d = nc.dram_tensor("ddT", [128, H * 2 * 256], BF16, kind="ExternalInput")
    qdm_d = nc.dram_tensor("qdm", [128, 16 * 256], BF16, kind="ExternalInput")
    kdm_d = nc.dram_tensor("kdm", [128, 2 * 2048], BF16, kind="ExternalInput")
    bdm_d = nc.dram_tensor("bdm", [128, 1024], F32, kind="ExternalInput")
    qd512_d = nc.dram_tensor("qd512", [128, 16 * 512], BF16, kind="ExternalInput")
    onec_d = nc.dram_tensor("onec", [128, 8], BF16, kind="ExternalInput")
    oner_d = nc.dram_tensor("oner", [1, 128], BF16, kind="ExternalInput")
    ident_d = nc.dram_tensor("ident", [128, 128], BF16, kind="ExternalInput")

    yo = nc.dram_tensor("yo", [128, NCH, T], F32, kind="ExternalOutput")

    q_st = nc.dram_tensor("q_st", [128, NCH, T], BF16, kind="Internal")
    y_st = nc.dram_tensor("y_st", [128, NCH, T], BF16, kind="Internal")
    wgu = nc.dram_tensor("wgu", [22, 16, 128, 512], BF16, kind="Internal")
    wdn = nc.dram_tensor("wdn", [DN_MC, 128, 44, 128], BF16, kind="Internal")
    fin = nc.dram_tensor("fin", [128, 1024], F32, kind="Internal")
    fgo = nc.dram_tensor("fgo", [256, 1024], F32, kind="Internal")

    dbg = {}
    dbg_specs = {"y": ([128, NCH, T], BF16), "q": ([128, NCH, T], BF16),
                 "F": ([128, 1024], F32), "wgu": ([22, 16, 128, 512], BF16),
                 "wdn": ([DN_MC, 128, 44, 128], BF16),
                 "kp": ([128, 4096], BF16), "vp": ([128, 4096], BF16),
                 "qp": ([128, 512], BF16), "S1": ([128, 1024], F32)}
    for name in debug:
        sh, dt = dbg_specs[name]
        dbg[name] = nc.dram_tensor("dbg_" + name, sh, dt, kind="ExternalOutput")

    es = ExitStack()
    AR_COLS = 100352
    arena = es.enter_context(nc.sbuf_tensor("arena", [128, AR_COLS], BF16))
    ar = Arena(arena, AR_COLS)

    ar.alloc("scgu", GU_MC * 16 * 2)
    ar.alloc("scdn", DN_MC * 44 * 2)
    ar.alloc("oner", 128)
    ar.alloc("ident", 128)
    ar.alloc("dq_pk", 2 * 256 * 2)
    ar.alloc("dq_t", 6 * 256 * 2)
    ar.alloc("dq_v", 2048 * 2)
    ar.alloc("dq_vb", 2048)
    ar.alloc("dq_tr", 1024)
    ar.regions["scst"] = (ar.regions["dq_v"][0], GU_MC * 16)
    p0_base = ar.off
    ar.alloc("ddr", 4 * 1024)
    ar.alloc("qdm", 16 * 256)
    ar.alloc("kdm", 2 * 2048)
    ar.alloc("bdm", 1024 * 2)
    ar.alloc("S", 1024 * 2)
    ar.alloc("Sbf", 1024)
    ar.alloc("qkv", 48 * TC)
    ar.alloc("wbf", 8 * 512)
    ar.alloc("wf32", 2 * 512 * 2)
    ar.alloc("xr", 2 * 512 * 2)
    ar.alloc("xb", NCH * TC)
    ar.alloc("kp", 2 * 2048)
    ar.alloc("vp", 2 * 2048)
    ar.alloc("qp", 2 * 256)
    ar.alloc("awb", 2 * 1024)
    ar.alloc("yr", 4 * 256)
    p0_end = ar.off
    ar.off = p0_base
    ar.alloc("x32", NCH * TT * 2)
    ar.alloc("xb1", NCH * TT)
    ar.alloc("yf", NCH * TT)
    ar.alloc("qr", 4 * TT)
    ar.alloc("zr", 2 * TT)
    ar.alloc("sqr", 2 * TT)
    ar.alloc("Fbf", 1024)
    ar.alloc("Fsc", 1024)
    ar.alloc("wat", 8 * 512 + 2 * 512 * 2)
    ar.alloc("gp", NCH * TT)
    ar.alloc("g1", 2 * TT * 2)
    ar.alloc("cmb", 2 * TT * 2)
    ar.alloc("ag", NCH * TT)
    ar.alloc("rows", 4096)
    ar.alloc("hs", 2 * 1024)
    ar.alloc("ybr", 2 * TT)
    p1_end = ar.off
    ar.off = max(p0_end, p1_end)
    ar.alloc("y32x", 3 * 2048)
    ar.alloc("dnw", 2 * 2816)
    ar.alloc("normw", NCH * 2)
    ar.alloc("ones", 8)
    assert ar.off <= AR_COLS, f"arena {ar.off} > {AR_COLS}"
    ar.regions["gw"] = (ar.regions["wat"][0], 2 * 2048)
    ar.regions["mrb"] = (ar.regions["dnw"][0], 2048)
    ar.regions["qd512"] = ar.regions["ag"]

    def y32slot(mc):
        if mc < 4:
            return ar.f32("xb1", 512, mc * 512)
        if mc < 8:
            return ar.f32("ag", 512, (mc - 4) * 512)
        if mc < 13:
            return ar.f32("wat", 512, (mc - 8) * 512)
        return ar.f32("y32x", 512, (mc - 13) * 512)

    H_REGS = [("gp", 16), ("yf", 16), ("qr", 4), ("zr", 2), ("sqr", 2), ("g1", 4)]

    def hslot(c):
        for rname, n in H_REGS:
            if c < n:
                return ar.bf(rname, 512, c * 512)
            c -= n
        raise AssertionError

    def rows_f32(slot, n=512):
        o = ar.regions["rows"][0]
        return arena[0:1, o + slot * 1024:o + slot * 1024 + 2 * n].bitcast(F32)

    def rhs_bf(off, n=512):
        o = ar.regions["rows"][0]
        return arena[0:1, o + 3072 + off:o + 3072 + off + n]

    PS = [es.enter_context(nc.psum_tensor(f"ps{i}", [128, 512], F32)) for i in range(6)]
    PSW = es.enter_context(nc.psum_tensor("psw", [128, 1024], F32))
    PSWb = PSW[:].bitcast(BF16)
    PS5b = PS[5][:].bitcast(BF16)

    sem_names = (["dS", "dA", "sV", "sT", "sA", "sP", "sC", "dC", "dK", "dV",
                  "dQS", "dOUT", "dYL", "dPK0", "dPK1", "dTR", "dDQ", "dQ5"]
                 + [f"dX{i}" for i in range(2)] + [f"dW{i}" for i in range(2)]
                 + [f"dDD{i}" for i in range(4)] + [f"dY{i}" for i in range(4)]
                 + [f"dQ{i}" for i in range(4)] + [f"dGW{i}" for i in range(2)]
                 + [f"dDN{i}" for i in range(2)])
    sems = {n: es.enter_context(nc.semaphore(n)) for n in sem_names}
    p = Plan()
    tok = p.tok

    def dma(queue, dst, src, waits=(), sem=None, transpose=False):
        qsem = sem or {"sync": "dS", "scalar": "dA"}[queue]
        p.do(queue, lambda e, d=dst, s=src, t=transpose: e.dma_start(d, s, transpose=t),
             waits, [(qsem, 16)])
        return tok(qsem)

    def cV(fn, waits=()):
        p.do("vector", fn, waits, [("sV", 1)])
        return tok("sV")

    def cT(fn, waits=()):
        p.do("tensor", fn, waits, [("sT", 1)])
        return tok("sT")

    def cA(fn, waits=()):
        p.do("scalar", fn, waits, [("sA", 1)])
        return tok("sA")

    def cP(fn, waits=()):
        p.do("gpsimd", fn, waits, [("sP", 1)])
        return tok("sP")

    onec = arena[:, ar.regions["ones"][0]:ar.regions["ones"][0] + 1]
    oner = arena[0:1, ar.regions["oner"][0]:ar.regions["oner"][0] + 128]
    ident = arena[:, ar.regions["ident"][0]:ar.regions["ident"][0] + 128]

    def Sbf_ap(h):
        o = ar.regions["Sbf"][0]
        r = (h % 2) * 64
        c = (h // 2) * 64
        return arena[r:r + 64, o + c:o + c + 64]

    def Fsc_ap(h):
        o = ar.regions["Fsc"][0]
        r = (h % 2) * 64
        c = (h // 2) * 64
        return arena[r:r + 64, o + c:o + c + 64]

    def PSW_S(h):
        r = (h % 2) * 64
        c = (h // 2) * 64
        return PSW[r:r + 64, c:c + 64]

    dma("sync", ar.bf("qdm"), qdm_d.ap(), sem="dC")
    dma("sync", ar.bf("kdm"), kdm_d.ap(), sem="dC")
    dma("sync", ar.f32("bdm"), bdm_d.ap(), sem="dC")
    dma("sync", ar.f32("normw"), normw.ap(), sem="dC")
    dma("sync", arena[:, ar.regions["ones"][0]:ar.regions["ones"][0] + 8], onec_d.ap(), sem="dC")
    dma("sync", arena[0:1, ar.regions["oner"][0]:ar.regions["oner"][0] + 128], oner_d.ap(), sem="dC")
    dma("sync", arena[:, ar.regions["ident"][0]:ar.regions["ident"][0] + 128], ident_d.ap(), sem="dC")
    scst16 = arena[:, ar.regions["scst"][0]:ar.regions["scst"][0] + GU_MC * 16].bitcast(F16)
    t_ld = dma("sync", scst16.rearrange("p (c s) -> p c s", s=16),
               gusc.ap().rearrange("(c p) s -> p c s", p=128), sem="dC")
    t_scgu = cV(lambda e: e.tensor_copy(ar.f32("scgu"), scst16), [t_ld])
    scst16d = arena[:, ar.regions["scst"][0]:ar.regions["scst"][0] + DN_MC * 44].bitcast(F16)
    t_ld2 = dma("sync", scst16d.rearrange("p (c s) -> p c s", s=44),
                dnsc.ap().rearrange("(c p) s -> p c s", p=128), waits=[t_scgu], sem="dC")
    t_scdn = cV(lambda e: e.tensor_copy(ar.f32("scdn"), scst16d), [t_ld2])
    t_S0 = cV(lambda e: e.memset(ar.f32("S"), 0.0))
    t_Sbf = cP(lambda e: e.memset(ar.bf("Sbf"), 0.0))
    t_consts = tok("dC")

    gu_done = [None] * GU_MC
    dn_done = [None] * (DN_MC * 4)
    dq = {"pk_par": 0, "pk_war": [None, None], "v_rd": None,
          "vb_rd": None, "tr_dma": None}

    def _dq_nib_dve(pk, W, extra_waits):
        vi = ar.i32("dq_v", 8 * W)
        ts_ = [ar.i32("dq_t", W, i * 256) for i in range(6)]
        w0 = list(extra_waits)
        for j in range(8):
            def f(e, j=j, pk=pk, W=W, vi=vi, ts_=ts_):
                t, u, sb, m1, pp, g = ts_
                e.tensor_scalar(t[:], pk, 4 * j, None, op0=ALU.logical_shift_right)
                e.tensor_scalar(u[:], t[:], 7, None, op0=ALU.bitwise_and)
                e.tensor_scalar(sb[:], t[:], 8, 28, op0=ALU.bitwise_and, op1=ALU.logical_shift_left)
                e.tensor_scalar(m1[:], u[:], 1, None, op0=ALU.min)
                e.scalar_tensor_tensor(pp[:], u[:], 251, m1[:], op0=ALU.add, op1=ALU.mult)
                e.tensor_scalar(g[:], u[:], 2, None, op0=ALU.is_ge)
                e.tensor_tensor(pp[:], pp[:], g[:], op=ALU.add)
                e.tensor_scalar(pp[:], pp[:], 22, None, op0=ALU.logical_shift_left)
                return e.tensor_tensor(vi.rearrange("p (w k) -> p w k", k=8)[:, :, j],
                                       pp[:], sb[:], op=ALU.bitwise_or)
            last = cV(f, w0)
            w0 = []
        return last, "sV"

    def _dq_nib_pool(pk, W, extra_waits):
        vf = ar.i32("dq_v", 8 * W).bitcast(F32)
        n_i = ar.i32("dq_t", W, 0)
        nf = ar.f32("dq_t", W, 256)
        s = ar.f32("dq_t", W, 512)
        u = ar.f32("dq_t", W, 768)
        r4 = ar.f32("dq_t", W, 1024)
        r6 = ar.f32("dq_t", W, 1280)
        w0 = list(extra_waits)
        last = None
        for j in range(8):
            t_n = cV(lambda e, j=j: e.tensor_scalar(
                n_i[:], pk, 4 * j, 15, op0=ALU.logical_shift_right, op1=ALU.bitwise_and),
                w0 + ([last] if last else []))
            w0 = []
            def fp(e, j=j):
                e.tensor_copy(nf[:], n_i[:])
                e.tensor_scalar(s[:], nf[:], 8.0, None, op0=ALU.is_ge)
                e.tensor_scalar(u[:], s[:], -8.0, None, op0=ALU.mult)
                e.tensor_tensor(u[:], u[:], nf[:], op=ALU.add)
                e.tensor_scalar(r4[:], u[:], -4.0, None, op0=ALU.add)
                e.tensor_scalar(r6[:], r4[:], -2.0, None, op0=ALU.add)
                e.tensor_scalar(r4[:], r4[:], 0.0, None, op0=ALU.max)
                e.tensor_scalar(r6[:], r6[:], 0.0, None, op0=ALU.max)
                e.tensor_tensor(r4[:], r4[:], u[:], op=ALU.add)
                e.tensor_tensor(r6[:], r6[:], r6[:], op=ALU.add)
                e.tensor_tensor(r4[:], r4[:], r6[:], op=ALU.add)
                e.tensor_scalar(r4[:], r4[:], 0.5, None, op0=ALU.mult)
                e.tensor_scalar(s[:], s[:], -2.0, None, op0=ALU.mult)
                e.tensor_scalar(s[:], s[:], 1.0, None, op0=ALU.add)
                return e.tensor_tensor(vf.rearrange("p (w k) -> p w k", k=8)[:, :, j],
                                       r4[:], s[:], op=ALU.mult)
            last = cP(fp, [t_n])
        return last, "sP"

    def _dq_chunk(pk_src, W, scview, goff, ngroups, grsz, dst, use_pool):
        par = dq["pk_par"]
        dq["pk_par"] ^= 1
        pk = ar.i32("dq_pk", W, par * 256)
        wt = [w for w in (dq["pk_war"][par],) if w]
        tpk = dma("scalar", pk, pk_src, waits=wt, sem=f"dPK{par}")
        extra = [tpk, tok("sV"), tok("sP")]
        if dq["v_rd"]:
            extra.append(dq["v_rd"])
        if use_pool:
            t_nib, _ = _dq_nib_pool(pk, W, extra)
        else:
            t_nib, _ = _dq_nib_dve(pk, W, extra)
        dq["pk_war"][par] = t_nib
        wv = [t_nib] + ([dq["vb_rd"]] if dq["vb_rd"] else [])
        for g in range(ngroups):
            def fs(e, g=g):
                return e.tensor_scalar(
                    ar.bf("dq_vb", grsz, g * grsz),
                    ar.i32("dq_v", 8 * W).bitcast(F32)[:, g * grsz:(g + 1) * grsz],
                    scview[:, goff + g:goff + g + 1], None, op0=ALU.mult)
            t_sc = cV(fs, wv)
            wv = []
        dq["v_rd"] = t_sc
        nkc = ngroups
        t_out = None
        for r in range((nkc + 7) // 8):
            kcs = list(range(r * 8, min(nkc, r * 8 + 8)))
            for kc in kcs:
                wtt = []
                if kc == kcs[0]:
                    wtt = [t_sc] + ([dq["tr_dma"]] if dq["tr_dma"] else [])
                def ftr(e, kc=kc):
                    return e.dma_start(ar.bf("dq_tr", 128, (kc % 8) * 128),
                                       ar.bf("dq_vb", 128, kc * 128), transpose=True)
                p.do("scalar", ftr, wtt, [("dTR", 16)])
            t_tr = tok("dTR")
            n = len(kcs)
            t_out = dma("scalar", dst[:, r * 8:r * 8 + n, :],
                        ar.bf("dq_tr", n * 128).rearrange("p (k m) -> p k m", m=128),
                        waits=[t_tr], sem="dDQ")
            dq["tr_dma"] = t_out
        dq["vb_rd"] = tok("dTR")
        return t_out

    def plan_dequant_gu(mc, use_pool):
        dst = wgu.ap()[mc // 4, :, :, (mc % 4) * 128:(mc % 4) * 128 + 128]
        dst = dst.rearrange("k p m -> p k m")
        t = _dq_chunk(gup.ap()[mc * 128:(mc + 1) * 128, :], GU_W,
                      ar.f32("scgu", 16, mc * 16), 0, 16, 128, dst, use_pool)
        gu_done[mc] = t

    def plan_dequant_dn(qq, use_pool):
        M, quar = qq // 4, qq % 4
        dst = wdn.ap()[M, :, quar * 11:quar * 11 + 11, :]
        t = _dq_chunk(dnp.ap()[M * 128:(M + 1) * 128, quar * 176:(quar + 1) * 176], 176,
                      ar.f32("scdn", 44, M * 44), quar * 11, 11, 128, dst, use_pool)
        dn_done[qq] = t

    dq_queue = []
    gi = di = 0
    while gi < GU_MC or di < DN_MC * 4:
        if di * 88 >= gi * 64 and gi < GU_MC or di >= DN_MC * 4:
            dq_queue.append(("gu", gi)); gi += 1
        else:
            dq_queue.append(("dn", di)); di += 1
    dq_pos = [0]
    POOL_SHARE = 0.45

    def pump_dequant(n):
        for _ in range(n):
            if dq_pos[0] >= len(dq_queue):
                return
            kind, idx = dq_queue[dq_pos[0]]
            use_pool = (dq_pos[0] % 20) < int(20 * POOL_SHARE)
            (plan_dequant_gu if kind == "gu" else plan_dequant_dn)(idx, use_pool)
            dq_pos[0] += 1

    qkvv = ar.bf("qkv")
    xbv = ar.bf("xb")
    kpv = ar.bf("kp")
    vpv = ar.bf("vp")

    t_xr = [None, None]
    t_wf = [None, None]
    t_wbf = [None] * 16
    t_wbf_rd = [None] * 16
    t_xb = [None] * 16
    t_ps_evac = [None] * 4
    t_qkv_rd_prev = None
    t_kp = None
    t_vp = None
    t_Supd = t_S0
    t_awdd = [None, None]
    t_inter = [None, None]
    t_yev = [None] * 4
    t_ydma = [None] * 4
    t_qp = [None, None]
    t_ds_last = None
    t_scan_aw_rd = [None, None, None, None]
    t_ddm_slot = [None] * 4
    t_dddma = [None] * 4

    def dd_issue(pr):
        slot = pr % 4
        t_dddma[slot] = dma("sync", ar.bf("ddr", 1024, slot * 1024),
                            ddT_d.ap()[:, pr * 1024:(pr + 1) * 1024],
                            waits=[w for w in (t_ddm_slot[slot],) if w], sem=f"dDD{slot}")

    for pr0 in range(4):
        dd_issue(pr0)

    NCC = 1 if stop_after in ("p0c1", "qkv", "tr", "aw", "qst", "trnq", "blk1") else NC0
    for cc in range(NCC):
        t0 = cc * TC
        for kc in range(NCH):
            xsl = ar.f32("xr", 512, (kc % 2) * 512)
            td = dma("sync", xsl, x_fm.ap()[:, kc, t0:t0 + TC],
                     waits=[w for w in (t_xr[kc % 2],) if w], sem=f"dX{kc % 2}")
            wts = [td] + ([t_qkv_rd_prev] if (kc < 2 and t_qkv_rd_prev) else [])
            t_xb[kc] = cP(lambda e, kc=kc, xs=xsl: e.tensor_copy(
                xbv[:, kc * TC:(kc + 1) * TC], xs), wts)
            t_xr[kc % 2] = t_xb[kc]
        for ms in range(12):
            for kh in range(2):
                for kc8 in range(8):
                    kc = kh * 8 + kc8
                    wsl = ar.f32("wf32", 512, (kc8 % 2) * 512)
                    td = dma("sync", wsl, qkvT.ap()[kc * 128:(kc + 1) * 128, ms * 512:(ms + 1) * 512],
                             waits=[w for w in (t_wf[kc8 % 2],) if w], sem=f"dW{kc8 % 2}")
                    t_wbf[kc8] = cA(lambda e, kc8=kc8, w=wsl: e.copy(
                        ar.bf("wbf", 512, kc8 * 512), w),
                        [td] + ([t_wbf_rd[kc8]] if t_wbf_rd[kc8] else []))
                    t_wf[kc8 % 2] = t_wbf[kc8]
                for kc8 in range(8):
                    kc = kh * 8 + kc8
                    for msub in range(4):
                        wts = [t_wbf[kc8], t_xb[kc]]
                        if kc == 0:
                            if t_ps_evac[msub]:
                                wts.append(t_ps_evac[msub])
                            if cc > 0 and ms == 0:
                                wts += [w for w in t_yev + t_awdd if w]
                        def fmm(e, kc=kc, kc8=kc8, msub=msub):
                            return e.matmul(PS[msub][:],
                                            ar.bf("wbf", 128, kc8 * 512 + msub * 128),
                                            xbv[:, kc * TC:(kc + 1) * TC],
                                            start=(kc == 0), stop=(kc == 15))
                        t_mm = cT(fmm, [w for w in wts if w])
                    t_wbf_rd[kc8] = t_mm
            for msub in range(4):
                mc = ms * 4 + msub
                wse = [t_mm]
                if ms == 0 and msub == 0 and cc > 0:
                    wse.append(tok("dQS"))
                t_ps_evac[msub] = cA(lambda e, msub=msub, mc=mc: e.activation(
                    qkvv[:, mc * TC:(mc + 1) * TC], PS[msub][:], AF.Silu), wse)
        t_qkv_rd_prev = t_mm
        t_qkv_done = tok("sA")
        if stop_after == "qkv":
            break
        if stop_after != "trnq":
            for mc in range(NCH):
                dma("scalar", q_st.ap()[:, mc, t0:t0 + TC], qkvv[:, mc * TC:(mc + 1) * TC],
                    waits=[t_qkv_done], sem="dQS")
        if stop_after == "qst":
            break
        for bi in range(2):
            wts0 = [t_qkv_done, t_Supd]
            if t_ds_last:
                wts0.append(t_ds_last)
            if t_kp:
                wts0.append(t_kp)
            for jh in range(2):
                for pr in range(16):
                    c = (16 + pr) * TC + bi * 256 + jh * 128
                    p.do("sync", lambda e, c=c, jh=jh, pr=pr: e.dma_start(
                        kpv[:, jh * 2048 + pr * 128:jh * 2048 + (pr + 1) * 128],
                        qkvv[:, c:c + 128], transpose=True),
                        wts0 if (jh == 0 and pr == 0) else (), [("dK", 16)])
            t_ktr = tok("dK")
            for jh in range(2):
                t_kp = cV(lambda e, jh=jh: e.tensor_tensor(
                    kpv[:, jh * 2048:(jh + 1) * 2048], kpv[:, jh * 2048:(jh + 1) * 2048],
                    ar.bf("kdm", 2048, jh * 2048), op=ALU.mult), [t_ktr] if jh == 0 else ())
            for jh in range(2):
                for pr in range(16):
                    c = (32 + pr) * TC + bi * 256 + jh * 128
                    p.do("sync", lambda e, c=c, jh=jh, pr=pr: e.dma_start(
                        vpv[:, jh * 2048 + pr * 128:jh * 2048 + (pr + 1) * 128],
                        qkvv[:, c:c + 128], transpose=True), (), [("dV", 16)])
            t_vp = tok("dV")
            def plan_awT(pr):
                ring = pr % 2
                for hh in range(2):
                    psp = PS[1 + ring * 2 + hh]
                    for jh in range(2):
                        wts = []
                        if hh == 0 and jh == 0:
                            wts = [w for w in (t_vp, t_scan_aw_rd[ring * 2],
                                               t_scan_aw_rd[ring * 2 + 1]) if w]
                        def fa(e, pr=pr, hh=hh, jh=jh, psp=psp, bi=bi):
                            c = (16 + pr) * TC + bi * 256 + jh * 128
                            q0 = pr * TC + bi * 256
                            return e.matmul(psp[:, jh * 256:jh * 256 + 256],
                                            qkvv[hh * 64:hh * 64 + 64, c:c + 128],
                                            qkvv[hh * 64:hh * 64 + 64, q0:q0 + 256],
                                            start=True, stop=True)
                        t_aw = cT(fa, wts)
                t_qp[pr % 2] = cP(lambda e, pr=pr, bi=bi: e.tensor_tensor(
                    ar.bf("qp", 256, (pr % 2) * 256),
                    qkvv[:, pr * TC + bi * 256:pr * TC + bi * 256 + 256],
                    ar.bf("qdm", 256, pr * 256), op=ALU.mult),
                    [t_qkv_done] + ([t_inter[pr % 2]] if t_inter[pr % 2] else []))
                return t_aw

            def plan_dd(pr, t_aw):
                ring = pr % 2
                slot = pr % 4
                for hh in range(2):
                    psp = PS[1 + ring * 2 + hh]
                    wts = [t_aw]
                    if hh == 0:
                        wts.append(t_dddma[slot])
                        if t_inter[ring]:
                            wts.append(t_inter[ring])
                    t_dd = cV(lambda e, pr=pr, hh=hh, psp=psp, ring=ring, slot=slot:
                              e.tensor_tensor(
                                  ar.bf("awb", 512, ring * 1024 + hh * 512), psp[:],
                                  ar.bf("ddr", 512, slot * 1024 + hh * 512), op=ALU.mult), wts)
                    t_scan_aw_rd[ring * 2 + hh] = t_dd
                t_awdd[ring] = t_dd
                t_ddm_slot[slot] = t_dd
                dd_issue((pr + 4) % 16)
                return t_dd

            def plan_y_dS(pr, t_dd):
                ring = pr % 2
                ytensor = PS[0] if ring == 0 else PS[5]
                first_wts = [t_dd, t_kp, t_vp, t_qp[ring], t_Sbf]
                if t_yev[(pr - 2) % 4]:
                    first_wts.append(t_yev[(pr - 2) % 4])
                first = True
                for hh in range(2):
                    h = 2 * pr + hh
                    for jh in range(2):
                        def fi(e, hh=hh, jh=jh, h=h, ring=ring, yt=ytensor):
                            return e.matmul(yt[hh * 64:hh * 64 + 64, 0:256],
                                            vpv[:, jh * 2048 + h * 64:jh * 2048 + h * 64 + 64],
                                            ar.bf("awb", 256, ring * 1024 + hh * 512 + jh * 256),
                                            start=(jh == 0), stop=False)
                        cT(fi, [w for w in first_wts if w] if first else [])
                        first = False
                    def fint(e, hh=hh, h=h, yt=ytensor):
                        return e.matmul(yt[hh * 64:hh * 64 + 64, 0:256],
                                        Sbf_ap(h),
                                        ar.bf("qp", 256, (0 if yt is PS[0] else 1) * 256)[hh * 64:hh * 64 + 64, :],
                                        start=False, stop=True)
                    t_y = cT(fint)
                t_inter[ring] = t_y
                ysl = ar.bf("yr", 256, (pr % 4) * 256)
                t_ev = cA(lambda e, ysl=ysl, yt=ytensor: e.copy(ysl, yt[:, 0:256]),
                    [t_y] + ([t_ydma[pr % 4]] if t_ydma[pr % 4] else []))
                t_yev[pr % 4] = t_ev
                t_ydma[pr % 4] = dma("scalar",
                                     y_st.ap()[:, pr, t0 + bi * 256:t0 + bi * 256 + 256],
                                     ysl, waits=[t_ev], sem=f"dY{pr % 4}")
                for hh in range(2):
                    h = 2 * pr + hh
                    for jh in range(2):
                        def fds(e, h=h, jh=jh):
                            return e.matmul(PSW_S(h),
                                            kpv[:, jh * 2048 + h * 64:jh * 2048 + h * 64 + 64],
                                            vpv[:, jh * 2048 + h * 64:jh * 2048 + h * 64 + 64],
                                            start=(jh == 0), stop=(jh == 1))
                        t_ds = cT(fds, [t_y] if (hh == 0 and jh == 0) else [])
                return t_ds

            if stop_after == "aw":
                for pr in range(16):
                    plan_dd(pr, plan_awT(pr))
                continue
            t_aw_p = plan_awT(0)
            t_dd_p = plan_dd(0, t_aw_p)
            for pr in range(1, 16):
                t_aw_n = plan_awT(pr)
                t_ds_last = plan_y_dS(pr - 1, t_dd_p)
                t_dd_p = plan_dd(pr, t_aw_n)
            t_ds_last = plan_y_dS(15, t_dd_p)
            cV(lambda e: e.tensor_tensor(ar.f32("S"), ar.f32("S"), ar.f32("bdm"), op=ALU.mult),
               [t_ds_last])
            t_Supd = cV(lambda e: e.tensor_tensor(ar.f32("S"), ar.f32("S"), PSW[:], op=ALU.add))
            t_Sbf = cP(lambda e: e.tensor_copy(ar.bf("Sbf"), ar.f32("S")), [t_Supd])
            if cc == 0 and bi == 0 and dbg:
                if "kp" in dbg:
                    dma("scalar", dbg["kp"].ap(), kpv[:], waits=[t_kp])
                if "vp" in dbg:
                    dma("scalar", dbg["vp"].ap(), vpv[:], waits=[t_vp])
                if "qp" in dbg:
                    dma("scalar", dbg["qp"].ap(), ar.bf("qp"), waits=[t_qp[0], t_qp[1]])
                if "S1" in dbg:
                    dma("scalar", dbg["S1"].ap(), ar.f32("S"), waits=[t_Supd])
            if stop_after == "blk1":
                break
        if dq_on:
            pump_dequant(19)
    if dq_on:
        pump_dequant(10 ** 6)

    if stop_after in ("p0c1", "qkv", "tr", "aw", "qst", "trnq", "blk1"):
        dbg_w0 = [tok(s) for s in sem_names if s.startswith("d")]
        if "y" in dbg:
            dma("scalar", dbg["y"].ap(), y_st.ap(), waits=dbg_w0)
        if "q" in dbg:
            dma("scalar", dbg["q"].ap(), q_st.ap(), waits=dbg_w0)
        final_waits = [tok(s) for s in sem_names if s.startswith("d")]
        p.do("vector", lambda e: e.engine_nop(), final_waits, ())
        with nc.Block() as block:
            block.sync(lambda e: p.emit("sync", e, sems))
            block.scalar(lambda e: p.emit("scalar", e, sems))
            block.vector(lambda e: p.emit("vector", e, sems))
            block.tensor(lambda e: p.emit("tensor", e, sems))
            block.gpsimd(lambda e: p.emit("gpsimd", e, sems))
        es.close()
        return nc
    t_fin = dma("scalar", fin.ap(), ar.f32("S"), waits=[t_Supd], sem="dC")
    p.do("gpsimd", lambda e: e.collective_compute(
        "AllGather", ALU.bypass, replica_groups=[[0, 1], [2, 3], [4, 5], [6, 7]],
        ins=[fin.ap().opt()], outs=[fgo.ap().opt()]), [t_fin], [("sC", 1)])
    t_coll = tok("sC")
    if "F" in dbg:
        dma("scalar", dbg["F"].ap(), ar.f32("S"), waits=[t_Supd])

    t_F = dma("sync", ar.f32("cmb", 1024), fgo.ap()[0:128, :], waits=[t_coll], sem="dC")
    t_Fbf = cP(lambda e: e.tensor_copy(ar.bf("Fbf"), ar.f32("cmb", 1024)), [t_F])

    slope, _, _, _, _ = _decay_tables()
    Avals = np.exp(-slope[:, None] * (512.0 * np.arange(NT1)[None, :])).astype(np.float32)

    t_qr = [None] * 4
    t_zr = [None, None]
    t_sqw = [None, None]
    t_yfw = [None] * 16
    t_corr = [None, None]
    t_gw = [None, None]
    t_dnw = [None, None]
    t_hs = [None, None]
    t_ybw = [None, None]
    t_watc = [None] * 16
    t_watr = [None] * 16
    t_wfat = [None, None]
    t_psw_user = None
    t_prev_out = None
    t_prev_x32_rd = None

    def newton(stat_ap, scale, eps, waits):
        v = rows_f32(0)
        y = rows_f32(1)
        t1 = rows_f32(2)
        cV(lambda e: e.tensor_scalar(v, stat_ap, scale, eps, op0=ALU.mult, op1=ALU.add), waits)
        cV(lambda e: e.tensor_scalar(y.bitcast(I32), v.bitcast(I32), 1, None,
                                     op0=ALU.logical_shift_right))
        cV(lambda e: e.tensor_scalar(y.bitcast(I32), y.bitcast(I32), -1, 0x5F3759DF,
                                     op0=ALU.mult, op1=ALU.add))
        tk = None
        for _ in range(3):
            cV(lambda e: e.tensor_tensor(t1, y, y, op=ALU.mult))
            cV(lambda e: e.tensor_tensor(t1, t1, v, op=ALU.mult))
            cV(lambda e: e.tensor_scalar(t1, t1, -0.5, 1.5, op0=ALU.mult, op1=ALU.add))
            tk = cV(lambda e: e.tensor_tensor(y, y, t1, op=ALU.mult))
        return tk, y

    def bcast(vals, waits, slot0=0):
        nonlocal t_psw_user
        n = len(vals)
        assert n == 1
        t_rhs = None
        for i, v in enumerate(vals):
            hi = rhs_bf(0, 512)
            lo = rhs_bf(512, 512)
            lov = rows_f32(2)
            cV(lambda e, v=v, hi=hi: e.tensor_copy(hi, v), waits if i == 0 else [])
            cV(lambda e, v=v, hi=hi, lov=lov: e.tensor_tensor(lov, v, hi, op=ALU.subtract))
            t_rhs = cV(lambda e, lo=lo, lov=lov: e.tensor_copy(lo, lov))
        outs = []
        t_last = None
        for i in range(n):
            for half in range(2):
                src = rhs_bf(half * 512, 512)
                wts = []
                if half == 0:
                    wts = ([t_rhs] if i == 0 else []) + ([t_psw_user] if t_psw_user else [])
                t_b = cT(lambda e, src=src, half=half: e.matmul(
                    PSW[:, half * 512:half * 512 + 512], oner, src, start=True, stop=True), wts)
            t_ev = cA(lambda e: e.copy(ar.f32("mrb", 1024), PSW[:]),
                      [t_b] + ([t_last] if t_last else []))
            dst = ar.f32("cmb", 512, slot0 * 512)
            t_last = cV(lambda e, dst=dst: e.tensor_tensor(
                dst, ar.f32("mrb", 512, 0), ar.f32("mrb", 512, 512), op=ALU.add), [t_ev])
            outs.append(dst)
            t_psw_user = t_ev
        return t_last, outs

    t_prev_down = None
    NTT = {"p0": 0, "p1t1": 1}.get(stop_after, NT1)
    for tl in range(NTT):
        t0 = tl * TT
        guard = [w for w in (t_prev_out, t_prev_down) if w]
        wx = list(guard) + ([t_prev_x32_rd] if t_prev_x32_rd else [])
        if tl == 0:
            wx = [t_fin, tok("sV"), tok("sT"), tok("sA"), tok("sP")]
        t_xd = dma("sync", ar.f32("x32").rearrange("p (c t) -> p c t", t=TT),
                   x_fm.ap()[:, :, t0:t0 + TT], waits=wx, sem="dX0")
        t_xbc = [None] * NCH
        for mc in range(NCH):
            wts = [t_xd]
            if mc == 0:
                wts += guard + ([tok("sT")] if tl > 0 else [])
            t_xbc[mc] = cP(lambda e, mc=mc: e.tensor_copy(
                ar.bf("xb1", 512, mc * 512), ar.f32("x32", 512, mc * 512)), wts)
        wq5 = list(guard)
        if tl == 0:
            wq5 = [tok("sT"), tok("sV"), tok("sA")]
        t_qd512 = dma("sync", ar.bf("qd512"), qd512_d.ap(), waits=wq5, sem="dQ5")
        t_Fsc = None
        for hg in range(2):
            for hc in range(16):
                hdx = 2 * hc + hg
                a = float(Avals[hdx, tl])
                wts = [t_Fbf] if (hg == 0 and hc == 0) else []
                if hg == 0 and hc == 0 and tl > 0:
                    wts.append(tok("sT"))
                t_Fsc = cP(lambda e, hg=hg, hc=hc, a=a: e.tensor_scalar(
                    Fsc_ap(2 * hc + hg), arena[hg * 64:hg * 64 + 64,
                                               ar.regions["Fbf"][0] + hc * 64:
                                               ar.regions["Fbf"][0] + hc * 64 + 64],
                    a, None, op0=ALU.mult), wts)
        t_statq = None
        for pr in range(16):
            wy = [w for w in (t_yfw[pr],) if w] + (guard if pr == 0 else [])
            if tl == 0 and pr == 0:
                wy += [tok("dY0"), tok("dY1"), tok("dY2"), tok("dY3")]
            dma("scalar", ar.bf("yf", 512, pr * 512), y_st.ap()[:, pr, t0:t0 + TT],
                waits=wy, sem="dYL")
        t_yall = tok("dYL")
        for pr in range(16):
            qsl = ar.bf("qr", 512, (pr % 4) * 512)
            wq = [w for w in (t_qr[pr % 4],) if w] + (guard if pr == 0 else [])
            if tl == 0 and pr == 0:
                wq.append(tok("dQS"))
            tq = dma("scalar", qsl, q_st.ap()[:, pr, t0:t0 + TT],
                     waits=wq, sem=f"dQ{pr % 4}")
            ysl = ar.bf("yf", 512, pr * 512)
            ty = t_yall
            zsl = ar.bf("zr", 512, (pr % 2) * 512)
            t_z = cP(lambda e, q=qsl, z=zsl, pr=pr: e.tensor_tensor(
                z, q, ar.bf("qd512", 512, pr * 512), op=ALU.mult),
                [tq, t_Fsc, t_qd512] + ([t_zr[pr % 2]] if t_zr[pr % 2] else [])
                + (guard if pr < 2 else []))
            t_zr[pr % 2] = t_z
            t_qr[pr % 4] = t_z
            cps = PS[3 + pr % 2]
            for hh in range(2):
                h = 2 * pr + hh
                wts = [t_z]
                if hh == 0:
                    if t_corr[pr % 2]:
                        wts.append(t_corr[pr % 2])
                    if pr < 2:
                        wts += guard
                    if tl == 0 and pr < 2:
                        wts += [w for w in t_awdd if w]
                t_cm = cT(lambda e, hh=hh, h=h, z=zsl, cps=cps: e.matmul(
                    cps[hh * 64:hh * 64 + 64, :], Fsc_ap(h),
                    z[hh * 64:hh * 64 + 64, :], start=True, stop=True), wts)
            t_yfull = cV(lambda e, y=ysl, cps=cps: e.tensor_tensor(y, y, cps[:], op=ALU.add),
                         [t_cm, ty])
            t_corr[pr % 2] = t_yfull
            t_yfw[pr] = t_yfull
            sqs = ar.bf("sqr", 512, (pr % 2) * 512)
            t_sq = cP(lambda e, y=ysl, s=sqs: e.tensor_tensor(s, y, y, op=ALU.mult),
                      [t_yfull] + ([t_sqw[pr % 2]] if t_sqw[pr % 2] else [])
                      + (guard if pr < 2 else []))
            wts = [t_sq]
            if pr == 0:
                wts.append(t_Supd)
                if t_psw_user:
                    wts.append(t_psw_user)
            t_statq = cT(lambda e, s=sqs, pr=pr: e.matmul(
                PSW[0:1, 0:512], onec, s, start=(pr == 0), stop=(pr == 15)), wts)
            t_sqw[pr % 2] = t_statq
        wts_n = [t_statq] + ([t_Fbf] if tl == 0 else [])
        t_n, rsd = newton(PSW[0:1, 0:512], 1.0 / DM, EPS_ATTN, wts_n)
        t_cmby, (rby,) = bcast([rsd], [t_n])
        gp_done = [None] * 16
        t_g1 = [None, None]
        t_bank_rd = [None] * 4
        for sl in range(4):
            for kh in range(2):
                for kc8 in range(8):
                    kc = kh * 8 + kc8
                    wsl = ar.f32("wat", 512, 2048 + (kc8 % 2) * 512)
                    wdd = [w for w in (t_wfat[kc8 % 2],) if w]
                    if sl == 0 and kh == 0 and kc8 < 2:
                        wdd += guard
                        if tl == 0:
                            wdd.append(tok("sA"))
                    td = dma("sync", wsl, gateT.ap()[kc * 128:(kc + 1) * 128, sl * 512:(sl + 1) * 512],
                             waits=wdd, sem=f"dW{kc8 % 2}")
                    wts = [td] + ([t_watr[kc8]] if t_watr[kc8] else [])
                    if sl == 0 and kh == 0 and kc8 < 2:
                        wts += guard
                    t_watc[kc8] = cA(lambda e, kc8=kc8, w=wsl: e.copy(
                        ar.bf("wat", 512, kc8 * 512), w), wts)
                    t_wfat[kc8 % 2] = t_watc[kc8]
                    for mcs in range(4):
                        mc = sl * 4 + mcs
                        wts = [t_watc[kc8], t_xbc[kc]]
                        if kc == 0 and sl > 0 and t_bank_rd[mcs]:
                            wts.append(t_bank_rd[mcs])
                        if kc == 0 and sl == 0:
                            if mcs == 3:
                                wts += [w for w in t_corr if w]
                            if tl == 0:
                                wts += [w for w in t_yev if w]
                                wts += [w for w in t_ps_evac if w]
                                wts += [w for w in t_awdd if w]
                        t_gmm = cT(lambda e, kc=kc, kc8=kc8, mcs=mcs, mc=mc: e.matmul(
                            PS[mcs][:], ar.bf("wat", 128, kc8 * 512 + mcs * 128),
                            ar.bf("xb1", 512, kc * 512), start=(kc == 0), stop=(kc == 15)),
                            [w for w in wts if w])
                    t_watr[kc8] = t_gmm
            for mcs in range(4):
                mc = sl * 4 + mcs
                g1s = ar.f32("g1", 512, (mc % 2) * 512)
                t_t = cA(lambda e, g=g1s, mcs=mcs: e.activation(g, PS[mcs][:], AF.Tanh, scale=0.5),
                         [t_gmm] + ([t_g1[mc % 2]] if t_g1[mc % 2] else [])
                         + (guard if mc < 2 else []))
                cV(lambda e, g=g1s: e.tensor_scalar(g, g, 0.5, 0.5, op0=ALU.mult, op1=ALU.add),
                   [t_t, t_cmby])
                t_g1[mc % 2] = cV(lambda e, mc=mc, g=g1s, rby=rby: e.tensor_tensor(
                    ar.bf("gp", 512, mc * 512), g, rby, op=ALU.mult),
                   (guard if mc < 2 else []))
                gp_done[mc] = t_g1[mc % 2]
                t_bank_rd[mcs] = t_t
        t_ag = [None] * 16
        for pr in range(16):
            t_ag[pr] = cV(lambda e, pr=pr: e.scalar_tensor_tensor(
                ar.bf("ag", 512, pr * 512), ar.bf("yf", 512, pr * 512),
                ar.f32("normw", 1, pr), ar.bf("gp", 512, pr * 512),
                op0=ALU.mult, op1=ALU.mult),
                [gp_done[pr], t_yfw[pr]] + (guard if pr == 0 else []))
        z1_done = [None] * 16
        for sl in range(4):
            for kh in range(2):
                for kc8 in range(8):
                    kc = kh * 8 + kc8
                    wsl = ar.f32("wat", 512, 2048 + (kc8 % 2) * 512)
                    td = dma("sync", wsl, outT.ap()[kc * 128:(kc + 1) * 128, sl * 512:(sl + 1) * 512],
                             waits=[w for w in (t_wfat[kc8 % 2],) if w], sem=f"dW{kc8 % 2}")
                    t_watc[kc8] = cA(lambda e, kc8=kc8, w=wsl: e.copy(
                        ar.bf("wat", 512, kc8 * 512), w),
                        [td] + ([t_watr[kc8]] if t_watr[kc8] else []))
                    t_wfat[kc8 % 2] = t_watc[kc8]
                    for mcs in range(4):
                        mc = sl * 4 + mcs
                        wts = [t_watc[kc8], t_ag[kc]]
                        if kc == 0 and t_bank_rd[mcs]:
                            wts.append(t_bank_rd[mcs])
                        t_omm = cT(lambda e, kc=kc, kc8=kc8, mcs=mcs, mc=mc: e.matmul(
                            PS[mcs][:], ar.bf("wat", 128, kc8 * 512 + mcs * 128),
                            ar.bf("ag", 512, kc * 512), start=(kc == 0), stop=(kc == 15)),
                            [w for w in wts if w])
                    t_watr[kc8] = t_omm
            for mcs in range(4):
                mc = sl * 4 + mcs
                z1_done[mc] = cV(lambda e, mc=mc, mcs=mcs: e.tensor_tensor(
                    ar.f32("x32", 512, mc * 512), ar.f32("x32", 512, mc * 512),
                    PS[mcs][:], op=ALU.add), [t_omm])
                t_bank_rd[mcs] = z1_done[mc]
        t_statmm = None
        for mc in range(NCH):
            sl2 = mc % 2
            t_zc = cA(lambda e, mc=mc, sl2=sl2: e.copy(
                ar.bf("hs", 512, sl2 * 1024), ar.f32("x32", 512, mc * 512)),
                [z1_done[mc]] + ([t_statmm] if mc >= 2 else ([t_psw_user] if t_psw_user else [])))
            t_zs = cP(lambda e, sl2=sl2: e.tensor_tensor(
                ar.bf("hs", 512, sl2 * 1024 + 512), ar.bf("hs", 512, sl2 * 1024),
                ar.bf("hs", 512, sl2 * 1024), op=ALU.mult), [t_zc])
            wts = [t_zs]
            if mc == 0 and t_psw_user:
                wts.append(t_psw_user)
            cT(lambda e, mc=mc, sl2=sl2: e.matmul(
                PSW[0:1, 0:512], onec, ar.bf("hs", 512, sl2 * 1024),
                start=(mc == 0), stop=(mc == NCH - 1)), wts)
            t_statmm = cT(lambda e, mc=mc, sl2=sl2: e.matmul(
                PSW[0:1, 512:1024], onec, ar.bf("hs", 512, sl2 * 1024 + 512),
                start=(mc == 0), stop=(mc == NCH - 1)))
        m_sv = ar.f32("mrb", 512, 0)[0:1, :]
        cV(lambda e: e.tensor_scalar(m_sv, PSW[0:1, 0:512], 1.0 / DM, None, op0=ALU.mult),
           [t_statmm])
        vv = rows_f32(0)
        cV(lambda e: e.tensor_scalar(vv, PSW[0:1, 512:1024], 1.0 / DM, None, op0=ALU.mult))
        t2 = rows_f32(2)
        cV(lambda e: e.tensor_tensor(t2, m_sv, m_sv, op=ALU.mult))
        t_var = cV(lambda e: e.tensor_tensor(vv, vv, t2, op=ALU.subtract))
        t_n1, rs1 = newton(vv, 1.0, EPS_LN, [t_var])
        t_cmbm, (mb1,) = bcast([m_sv], [t_n1], slot0=0)
        t_cmb1, (rb1,) = bcast([rs1], [t_cmbm], slot0=1)
        t_x1b = [None] * 16
        for mc in range(NCH):
            cV(lambda e, mc=mc, mb=mb1: e.tensor_tensor(
                ar.f32("x32", 512, mc * 512), ar.f32("x32", 512, mc * 512), mb, op=ALU.subtract),
                [t_cmb1] if mc == 0 else [])
            t_apq = cV(lambda e, mc=mc, rb=rb1: e.tensor_tensor(
                ar.f32("x32", 512, mc * 512), ar.f32("x32", 512, mc * 512), rb, op=ALU.mult))
            t_x1b[mc] = cP(lambda e, mc=mc: e.tensor_copy(
                ar.bf("xb1", 512, mc * 512), ar.f32("x32", 512, mc * 512)),
                [t_apq] + ([t_watr[15], tok("sT")] if mc == 0 else []))
        h_done = [None] * 44
        t_sil_prev = None
        for sl in range(44):
            twchunk = [None, None]
            for gu_i in range(2):
                mcw = 2 * sl + gu_i
                gsl = ar.bf("gw", 2048, gu_i * 2048)
                wts = [w for w in (t_gw[gu_i], gu_done[mcw]) if w]
                if sl == 0:
                    wts += [w for w in t_watr if w]
                    wts.append(tok("sA"))
                twchunk[gu_i] = dma(
                    "sync", gsl.rearrange("p (k m) -> p k m", m=128),
                    wgu.ap()[mcw // 4, :, :, (mcw % 4) * 128:(mcw % 4) * 128 + 128]
                    .rearrange("k p m -> p k m"), waits=wts, sem=f"dGW{gu_i}")
            for gu_i in range(2):
                ps_t = PS[gu_i]
                gsl = ar.bf("gw", 2048, gu_i * 2048)
                for kc in range(16):
                    wts = [twchunk[gu_i]]
                    if kc == 0:
                        wts.append(t_x1b[15])
                        if gu_i == 0 and t_sil_prev:
                            wts.append(t_sil_prev)
                        if gu_i == 1 and h_done[sl - 1]:
                            wts.append(h_done[sl - 1])
                        if sl == 0:
                            wts += [w for w in (t_bank_rd[0], t_bank_rd[1]) if w]
                    t_gu = cT(lambda e, kc=kc, ps_t=ps_t, gsl=gsl: e.matmul(
                        ps_t[:], gsl[:, kc * 128:kc * 128 + 128],
                        ar.bf("xb1", 512, kc * 512), start=(kc == 0), stop=(kc == 15)),
                        [w for w in wts if w])
                t_gw[gu_i] = t_gu
            hss = ar.bf("hs", 512, (sl % 2) * 1024)
            t_sil = cA(lambda e, h=hss: e.activation(h, PS[0][:], AF.Silu),
                       [t_gu] + ([t_hs[sl % 2]] if t_hs[sl % 2] else []))
            t_hs[sl % 2] = t_sil
            t_sil_prev = t_sil
            h_done[sl] = cV(lambda e, sl=sl, h=hss: e.tensor_tensor(
                hslot(sl), h, PS[1][:], op=ALU.mult),
                [t_sil] + (guard if sl < 2 else []))
        y32_done = [None] * 16
        t_stat2mm = None
        for M in range(DN_MC):
            t_dmm = None
            for half in range(2):
                slot = (M * 2 + half) % 2
                dsl = ar.bf("dnw", 2816, slot * 2816)
                wts = [w for w in (t_dnw[slot], dn_done[4 * M + 2 * half + 1]) if w]
                if M == 0 and half == 0:
                    wts += guard + [tok("sV")]
                tdn = dma("sync", dsl.rearrange("p (c m) -> p c m", m=128),
                          wdn.ap()[M, :, half * 22:(half + 1) * 22, :], waits=wts, sem=f"dDN{slot}")
                for ci in range(22):
                    c = half * 22 + ci
                    wts = [tdn, h_done[c]]
                    if c == 0:
                        if M >= 2 and y32_done[M - 2]:
                            wts.append(y32_done[M - 2])
                        if M < 2:
                            wts += [w for w in t_hs if w]
                            wts.append(h_done[43])
                    t_dmm = cT(lambda e, c=c, ci=ci, d=dsl, M=M: e.matmul(
                        PS[M % 2][:], d[:, ci * 128:ci * 128 + 128], hslot(c),
                        start=(c == 0), stop=(c == 43)), [w for w in wts if w])
                t_dnw[slot] = t_dmm
            t_ye = cA(lambda e, M=M: e.copy(y32slot(M), PS[M % 2][:]),
                      [t_dmm] + (guard if M == 0 else []))
            y32_done[M] = t_ye
            ybs = ar.bf("ybr", 512, (M % 2) * 512)
            cP(lambda e, M=M, y=ybs: e.tensor_copy(y, y32slot(M)),
               [t_ye] + ([t_ybw[M % 2]] if t_ybw[M % 2] else [])
               + (guard if M < 2 else []))
            t_yb = cP(lambda e, y=ybs: e.tensor_tensor(y, y, y, op=ALU.mult))
            wts = [t_yb]
            if M == 0 and t_psw_user:
                wts.append(t_psw_user)
            t_stat2mm = cT(lambda e, y=ybs, M=M: e.matmul(
                PSW[0:1, 0:512], onec, y, start=(M == 0), stop=(M == DN_MC - 1)), wts)
            t_ybw[M % 2] = t_stat2mm
        t_n2, rs2 = newton(PSW[0:1, 0:512], 1.0 / DM, EPS_FFN, [t_stat2mm])
        t_cmb2, (r2b,) = bcast([rs2], [t_n2])
        z2_done = [None] * 16
        for M in range(NCH):
            cV(lambda e, M=M, rb=r2b: e.tensor_tensor(y32slot(M), y32slot(M), rb, op=ALU.mult),
               [t_cmb2, y32_done[M]] if M == 0 else [y32_done[M]])
            z2_done[M] = cV(lambda e, M=M: e.tensor_tensor(
                y32slot(M), y32slot(M), ar.f32("x32", 512, M * 512), op=ALU.add))
        t_statmm = None
        for mc in range(NCH):
            sl2 = mc % 2
            t_zc = cA(lambda e, mc=mc, sl2=sl2: e.copy(
                ar.bf("hs", 512, sl2 * 1024), y32slot(mc)),
                [z2_done[mc]] + ([t_statmm] if mc >= 2 else [t_psw_user]))
            t_zs = cP(lambda e, sl2=sl2: e.tensor_tensor(
                ar.bf("hs", 512, sl2 * 1024 + 512), ar.bf("hs", 512, sl2 * 1024),
                ar.bf("hs", 512, sl2 * 1024), op=ALU.mult), [t_zc])
            wts = [t_zs]
            if mc == 0:
                wts.append(t_psw_user)
            cT(lambda e, mc=mc, sl2=sl2: e.matmul(
                PSW[0:1, 0:512], onec, ar.bf("hs", 512, sl2 * 1024),
                start=(mc == 0), stop=(mc == NCH - 1)), wts)
            t_statmm = cT(lambda e, mc=mc, sl2=sl2: e.matmul(
                PSW[0:1, 512:1024], onec, ar.bf("hs", 512, sl2 * 1024 + 512),
                start=(mc == 0), stop=(mc == NCH - 1)))
        m_sv = ar.f32("mrb", 512, 0)[0:1, :]
        cV(lambda e: e.tensor_scalar(m_sv, PSW[0:1, 0:512], 1.0 / DM, None, op0=ALU.mult),
           [t_statmm])
        vv = rows_f32(0)
        cV(lambda e: e.tensor_scalar(vv, PSW[0:1, 512:1024], 1.0 / DM, None, op0=ALU.mult))
        t2 = rows_f32(2)
        cV(lambda e: e.tensor_tensor(t2, m_sv, m_sv, op=ALU.mult))
        t_var = cV(lambda e: e.tensor_tensor(vv, vv, t2, op=ALU.subtract))
        t_n3, rs3 = newton(vv, 1.0, EPS_LN, [t_var])
        t_cmbm3, (mb3,) = bcast([m_sv], [t_n3], slot0=0)
        t_cmb3, (rb3,) = bcast([rs3], [t_cmbm3], slot0=1)
        t_last_out = None
        t_last_x32rd = None
        for M in range(NCH):
            cV(lambda e, M=M, mb=mb3: e.tensor_tensor(
                y32slot(M), y32slot(M), mb, op=ALU.subtract), [t_cmb3] if M == 0 else [])
            t_o = cV(lambda e, M=M, rb=rb3: e.tensor_tensor(
                ar.f32("x32", 512, M * 512), y32slot(M), rb, op=ALU.mult))
            t_last_x32rd = t_o
            t_last_out = dma("scalar", yo.ap()[:, M, t0:t0 + TT],
                             ar.f32("x32", 512, M * 512), waits=[t_o], sem="dOUT")
        t_prev_out = t_last_out
        t_prev_x32_rd = t_last_x32rd
        t_prev_down = t_dnw[1] or t_dnw[0]

    dbg_w = [tok(s) for s in sem_names if s.startswith("d")]
    for name in dbg:
        if name == "y":
            dma("scalar", dbg["y"].ap(), y_st.ap(), waits=dbg_w)
        elif name == "q":
            dma("scalar", dbg["q"].ap(), q_st.ap(), waits=dbg_w)
        elif name == "wgu":
            dma("scalar", dbg["wgu"].ap(), wgu.ap(), waits=dbg_w)
        elif name == "wdn":
            dma("scalar", dbg["wdn"].ap(), wdn.ap(), waits=dbg_w)

    final_waits = [tok(s) for s in sem_names if s.startswith("d")]
    p.do("vector", lambda e: e.engine_nop(), final_waits, ())

    with nc.Block() as block:
        block.sync(lambda e: p.emit("sync", e, sems))
        block.scalar(lambda e: p.emit("scalar", e, sems))
        block.vector(lambda e: p.emit("vector", e, sems))
        block.tensor(lambda e: p.emit("tensor", e, sems))
        block.gpsimd(lambda e: p.emit("gpsimd", e, sems))
    es.close()
    return nc



def _host_prep(inputs):
    x = np.asarray(inputs["L0_attn_hidden_states"])
    qkv_w = np.asarray(inputs["L0_attn_qkv_w"])
    out_w = np.asarray(inputs["L0_attn_out_w"])
    gate_w = np.asarray(inputs["L0_attn_gate_w"])
    norm_w = np.asarray(inputs["L0_attn_norm_w"])
    gupk = np.asarray(inputs["L0_ffn_gate_up_packed"])
    gusc = np.asarray(inputs["L0_ffn_gate_up_scales"])
    dnp_ = np.asarray(inputs["L0_ffn_down_packed"])
    dnsc = np.asarray(inputs["L0_ffn_down_scales"])

    hidx = np.arange(H)[:, None] * 192 + np.arange(64)[None, :]
    order = np.concatenate([(hidx + g * 64).reshape(-1) for g in range(3)])
    qkvT = np.ascontiguousarray(qkv_w[order].T)
    gateT = np.ascontiguousarray(gate_w.T)
    outT = np.ascontiguousarray(out_w.T)
    normw = np.ascontiguousarray(norm_w.reshape(NCH, 128).T)

    ridx = np.empty(2 * ID_, np.int64)
    for c in range(44):
        ridx[(2 * c) * 128:(2 * c) * 128 + 128] = np.arange(c * 128, c * 128 + 128)
        ridx[(2 * c + 1) * 128:(2 * c + 1) * 128 + 128] = np.arange(ID_ + c * 128, ID_ + c * 128 + 128)
    gupr = np.ascontiguousarray(gupk[ridx])
    guscT = np.ascontiguousarray(gusc.T[ridx])
    dnscT = np.ascontiguousarray(dnsc.T)

    onec = np.ones((128, 8), BF)
    oner = np.ones((1, 128), BF)
    ident = np.eye(128, dtype=BF)

    in_maps = []
    for c in range(8):
        b, half = c // 2, c % 2
        xh = x[b, half * T:(half + 1) * T, :]
        x_fmh = np.ascontiguousarray(xh.T.reshape(NCH, 128, T).transpose(1, 0, 2))
        consts = _build_consts(1.0 if half == 1 else 0.0)
        in_maps.append(dict(
            x_fm=x_fmh, qkvT=qkvT, gateT=gateT, outT=outT, normw=normw,
            gup=gupr, gusc=guscT, dnp=dnp_, dnsc=dnscT,
            ddT=consts["ddT"], qdm=consts["qdm"], kdm=consts["kdm"],
            bdm=consts["bdm"], qd512=consts["qd512"],
            onec=onec, oner=oner, ident=ident,
        ))
    return in_maps


def kernel(**inputs):
    in_maps = _host_prep(inputs)
    nc = build_nc()
    res = run_bass_kernel_spmd(nc, in_maps, core_ids=list(range(8)))
    out = np.empty((4, 2 * T, DM), np.float32)
    for c in range(8):
        b, half = c // 2, c % 2
        yov = np.asarray(res.results[c]["yo"])
        out[b, half * T:(half + 1) * T, :] = yov.transpose(2, 1, 0).reshape(T, DM)
    return out


# version 04 (final)
# speedup vs baseline: 2.0405x; 2.0405x over previous
import numpy as np
import ml_dtypes
from contextlib import ExitStack

import concourse.bass as bass
import concourse.mybir as mybir
from concourse.bass_utils import run_bass_kernel_spmd

F32 = mybir.dt.float32
BF16 = mybir.dt.bfloat16
F16 = mybir.dt.float16
I32 = mybir.dt.int32
AF = mybir.ActivationFunctionType
ALU = mybir.AluOpType
BF = ml_dtypes.bfloat16

H, D, BS, DM = 32, 64, 256, 2048
T = 4096
NBLK = T // BS
NCH = DM // 128
TC = 512
NC0 = T // TC
TT = 512
NT1 = T // TT
GU_MC = 88
GU_W = 256
DN_MC = 16
DN_W = 704
ID_ = 5632
EPS_ATTN = 1e-5
EPS_FFN = 1e-6
EPS_LN = 1e-5


def _decay_tables():
    h = np.arange(H, dtype=np.float64)
    slope = (0.5 ** (0.25 * (h + 1.0))) * (1.0 - 0.0 / (31 + 1e-5) + 1e-5)
    i = np.arange(BS, dtype=np.float64)
    qd = np.exp(-slope[:, None] * (i[None, :] + 1.0))
    kd = np.exp(-slope[:, None] * (BS - (i[None, :] + 1.0)))
    bd = np.exp(-slope * BS)
    dif = i[:, None] - i[None, :]
    dd = np.where(dif >= 0, np.exp(-slope[:, None, None] * dif[None]), 0.0)
    return slope, qd, kd, bd, dd


def _build_consts(mask_c):
    slope, qd, kd, bd, dd = _decay_tables()
    ddT = np.zeros((128, H * 2 * 256), np.float32)
    for pr in range(16):
        for hh in range(2):
            hd = 2 * pr + hh
            for jh in range(2):
                c0 = ((pr * 2 + hh) * 2 + jh) * 256
                ddT[:, c0:c0 + 256] = dd[hd, :, jh * 128:(jh + 1) * 128].T
    qdm = np.zeros((128, 16 * 256), np.float32)
    for pr in range(16):
        qdm[0:64, pr * 256:(pr + 1) * 256] = qd[2 * pr][None, :]
        qdm[64:128, pr * 256:(pr + 1) * 256] = qd[2 * pr + 1][None, :]
    kdm = np.zeros((128, 2 * 2048), np.float32)
    for jh in range(2):
        for pr in range(16):
            for hh in range(2):
                col = jh * 2048 + pr * 128 + hh * 64
                kdm[:, col:col + 64] = kd[2 * pr + hh, jh * 128:(jh + 1) * 128][:, None]
    bdm = np.zeros((128, 1024), np.float32)
    for hg in range(2):
        for hc in range(16):
            bdm[hg * 64:(hg + 1) * 64, hc * 64:(hc + 1) * 64] = bd[2 * hc + hg]
    tfull = np.arange(512, dtype=np.float64)
    qdf = np.exp(-slope[:, None] * (tfull[None, :] + 1.0))
    qd512 = np.zeros((128, 16 * 512), np.float32)
    for pr in range(16):
        qd512[0:64, pr * 512:pr * 512 + 512] = mask_c * qdf[2 * pr][None, :]
        qd512[64:128, pr * 512:pr * 512 + 512] = mask_c * qdf[2 * pr + 1][None, :]
    return dict(ddT=ddT.astype(BF), qdm=qdm.astype(BF), kdm=kdm.astype(BF),
                bdm=bdm, qd512=qd512.astype(BF))


ENGS = ("sync", "scalar", "vector", "tensor", "gpsimd")


class Plan:
    def __init__(self):
        self.acts = {e: [] for e in ENGS}
        self.counts = {}

    def do(self, eng, fn, waits=(), incs=()):
        ws = tuple((s, v) for w in waits if w is not None for (s, v) in (w,) if v > 0)
        self.acts[eng].append((fn, ws, tuple(incs)))
        for s, d in incs:
            self.counts[s] = self.counts.get(s, 0) + d

    def tok(self, sem):
        return (sem, self.counts.get(sem, 0))

    def emit(self, eng_name, eng, sems):
        hwm = {}
        for fn, waits, incs in self.acts[eng_name]:
            for s, v in waits:
                if hwm.get(s, 0) < v:
                    eng.wait_ge(sems[s], v)
                    hwm[s] = v
            inst = fn(eng)
            for s, d in incs:
                inst = inst.then_inc(sems[s], d)


class Arena:
    def __init__(self, tile, total_cols):
        self.t = tile
        self.total = total_cols
        self.off = 0
        self.regions = {}

    def alloc(self, name, cols):
        cols = (cols + 1) & ~1
        assert self.off + cols <= self.total, f"arena overflow at {name}"
        self.regions[name] = (self.off, cols)
        self.off += cols
        return self.regions[name]

    def bf(self, name, cols=None, off=0):
        o, c = self.regions[name]
        if cols is None:
            cols = c - off
        assert off + cols <= c, f"bf {name} oob"
        return self.t[:, o + off:o + off + cols]

    def bfr(self, name, rows, cols, off=0):
        o, c = self.regions[name]
        assert off + cols <= c
        return self.t[0:rows, o + off:o + off + cols]

    def f32(self, name, cols=None, off=0):
        o, c = self.regions[name]
        if cols is None:
            cols = c // 2 - off
        assert 2 * (off + cols) <= c, f"f32 {name} oob"
        return self.t[:, o + 2 * off:o + 2 * (off + cols)].bitcast(F32)

    def i32(self, name, cols=None, off=0):
        o, c = self.regions[name]
        if cols is None:
            cols = c // 2 - off
        assert 2 * (off + cols) <= c, f"i32 {name} oob"
        return self.t[:, o + 2 * off:o + 2 * (off + cols)].bitcast(I32)


def build_nc(debug=(), stop_after=None, dq_on=True):
    nc = bass.Bass("TRN2", target_bir_lowering=False)

    x_fm = nc.dram_tensor("x_fm", [128, NCH, T], F32, kind="ExternalInput")
    qkvT = nc.dram_tensor("qkvT", [DM, 3 * DM], F32, kind="ExternalInput")
    gateT = nc.dram_tensor("gateT", [DM, DM], F32, kind="ExternalInput")
    outT = nc.dram_tensor("outT", [DM, DM], F32, kind="ExternalInput")
    normw = nc.dram_tensor("normw", [128, NCH], F32, kind="ExternalInput")
    gup = nc.dram_tensor("gup", [2 * ID_, GU_W], I32, kind="ExternalInput")
    gusc = nc.dram_tensor("gusc", [2 * ID_, 16], F16, kind="ExternalInput")
    dnp = nc.dram_tensor("dnp", [DM, DN_W], I32, kind="ExternalInput")
    dnsc = nc.dram_tensor("dnsc", [DM, 44], F16, kind="ExternalInput")
    ddT_d = nc.dram_tensor("ddT", [128, H * 2 * 256], BF16, kind="ExternalInput")
    qdm_d = nc.dram_tensor("qdm", [128, 16 * 256], BF16, kind="ExternalInput")
    kdm_d = nc.dram_tensor("kdm", [128, 2 * 2048], BF16, kind="ExternalInput")
    bdm_d = nc.dram_tensor("bdm", [128, 1024], F32, kind="ExternalInput")
    qd512_d = nc.dram_tensor("qd512", [128, 16 * 512], BF16, kind="ExternalInput")
    onec_d = nc.dram_tensor("onec", [128, 8], BF16, kind="ExternalInput")
    oner_d = nc.dram_tensor("oner", [1, 128], BF16, kind="ExternalInput")
    ident_d = nc.dram_tensor("ident", [128, 128], BF16, kind="ExternalInput")

    yo = nc.dram_tensor("yo", [128, NCH, T], F32, kind="ExternalOutput")

    q_st = nc.dram_tensor("q_st", [128, NCH, T], BF16, kind="Internal")
    y_st = nc.dram_tensor("y_st", [128, NCH, T], BF16, kind="Internal")
    wgu = nc.dram_tensor("wgu", [22, 16, 128, 512], BF16, kind="Internal")
    wdn = nc.dram_tensor("wdn", [DN_MC, 128, 44, 128], BF16, kind="Internal")
    fin = nc.dram_tensor("fin", [128, 1024], F32, kind="Internal")
    fgo = nc.dram_tensor("fgo", [256, 1024], F32, kind="Internal")

    dbg = {}
    dbg_specs = {"y": ([128, NCH, T], BF16), "q": ([128, NCH, T], BF16),
                 "F": ([128, 1024], F32), "wgu": ([22, 16, 128, 512], BF16),
                 "wdn": ([DN_MC, 128, 44, 128], BF16),
                 "kp": ([128, 4096], BF16), "vp": ([128, 4096], BF16),
                 "qp": ([128, 512], BF16), "S1": ([128, 1024], F32)}
    for name in debug:
        sh, dt = dbg_specs[name]
        dbg[name] = nc.dram_tensor("dbg_" + name, sh, dt, kind="ExternalOutput")

    es = ExitStack()
    AR_COLS = 100352
    arena = es.enter_context(nc.sbuf_tensor("arena", [128, AR_COLS], BF16))
    ar = Arena(arena, AR_COLS)

    ar.alloc("scgu", GU_MC * 16 * 2)
    ar.alloc("scdn", DN_MC * 44 * 2)
    ar.alloc("oner", 128)
    ar.alloc("ident", 128)
    ar.alloc("dq_pk", 2 * 256 * 2)
    ar.alloc("dq_t", 6 * 256 * 2)
    ar.alloc("dq_v", 2048 * 2)
    ar.alloc("dq_vb", 2048)
    ar.alloc("dq_tr", 1024)
    ar.regions["scst"] = (ar.regions["dq_v"][0], GU_MC * 16)
    p0_base = ar.off
    ar.alloc("ddr", 4 * 1024)
    ar.alloc("qdm", 16 * 256)
    ar.alloc("kdm", 2 * 2048)
    ar.alloc("bdm", 1024 * 2)
    ar.alloc("S", 1024 * 2)
    ar.alloc("Sbf", 1024)
    ar.alloc("qkv", 48 * TC)
    ar.alloc("wbf", 8 * 512)
    ar.alloc("wf32", 2 * 512 * 2)
    ar.alloc("xr", 2 * 512 * 2)
    ar.alloc("xb", NCH * TC)
    ar.alloc("kp", 2 * 2048)
    ar.alloc("vp", 2 * 2048)
    ar.alloc("qp", 2 * 256)
    ar.alloc("awb", 2 * 1024)
    ar.alloc("yr", 4 * 256)
    p0_end = ar.off
    ar.off = p0_base
    ar.alloc("x32", NCH * TT * 2)
    ar.alloc("xb1", NCH * TT)
    ar.alloc("yf", NCH * TT)
    ar.alloc("qr", 4 * TT)
    ar.alloc("zr", 2 * TT)
    ar.alloc("sqr", 2 * TT)
    ar.alloc("Fbf", 1024)
    ar.alloc("Fsc", 1024)
    ar.alloc("wat", 8 * 512 + 2 * 512 * 2)
    ar.alloc("gp", NCH * TT)
    ar.alloc("g1", 2 * TT * 2)
    ar.alloc("cmb", 2 * TT * 2)
    ar.alloc("ag", NCH * TT)
    ar.alloc("rows", 4096)
    ar.alloc("hs", 2 * 1024)
    ar.alloc("ybr", 2 * TT)
    p1_end = ar.off
    ar.off = max(p0_end, p1_end)
    ar.alloc("y32x", 3 * 2048)
    ar.alloc("dnw", 2 * 2816)
    ar.alloc("normw", NCH * 2)
    ar.alloc("ones", 8)
    assert ar.off <= AR_COLS, f"arena {ar.off} > {AR_COLS}"
    ar.regions["gw"] = (ar.regions["wat"][0], 2 * 2048)
    ar.regions["mrb"] = (ar.regions["dnw"][0], 2048)
    ar.regions["qd512"] = ar.regions["ag"]

    def y32slot(mc):
        if mc < 4:
            return ar.f32("xb1", 512, mc * 512)
        if mc < 8:
            return ar.f32("ag", 512, (mc - 4) * 512)
        if mc < 13:
            return ar.f32("wat", 512, (mc - 8) * 512)
        return ar.f32("y32x", 512, (mc - 13) * 512)

    H_REGS = [("gp", 16), ("yf", 16), ("qr", 4), ("zr", 2), ("sqr", 2), ("g1", 4)]

    def hslot(c):
        for rname, n in H_REGS:
            if c < n:
                return ar.bf(rname, 512, c * 512)
            c -= n
        raise AssertionError

    def rows_f32(slot, n=512):
        o = ar.regions["rows"][0]
        return arena[0:1, o + slot * 1024:o + slot * 1024 + 2 * n].bitcast(F32)

    def rhs_bf(off, n=512):
        o = ar.regions["rows"][0]
        return arena[0:1, o + 3072 + off:o + 3072 + off + n]

    PS = [es.enter_context(nc.psum_tensor(f"ps{i}", [128, 512], F32)) for i in range(6)]
    PSW = es.enter_context(nc.psum_tensor("psw", [128, 1024], F32))
    PSWb = PSW[:].bitcast(BF16)
    PS5b = PS[5][:].bitcast(BF16)

    sem_names = (["dS", "dA", "sV", "sT", "sA", "sP", "sC", "dC", "dK", "dV",
                  "dQS", "dOUT", "dYL", "dPK0", "dPK1", "dTR", "dDQ", "dQ5"]
                 + [f"dX{i}" for i in range(2)] + [f"dW{i}" for i in range(2)]
                 + [f"dDD{i}" for i in range(4)] + [f"dY{i}" for i in range(4)]
                 + [f"dQ{i}" for i in range(4)] + [f"dGW{i}" for i in range(2)]
                 + [f"dDN{i}" for i in range(2)])
    sems = {n: es.enter_context(nc.semaphore(n)) for n in sem_names}
    p = Plan()
    tok = p.tok

    def dma(queue, dst, src, waits=(), sem=None, transpose=False):
        qsem = sem or {"sync": "dS", "scalar": "dA"}[queue]
        p.do(queue, lambda e, d=dst, s=src, t=transpose: e.dma_start(d, s, transpose=t),
             waits, [(qsem, 16)])
        return tok(qsem)

    def cV(fn, waits=()):
        p.do("vector", fn, waits, [("sV", 1)])
        return tok("sV")

    def cT(fn, waits=()):
        p.do("tensor", fn, waits, [("sT", 1)])
        return tok("sT")

    def cA(fn, waits=()):
        p.do("scalar", fn, waits, [("sA", 1)])
        return tok("sA")

    def cP(fn, waits=()):
        p.do("gpsimd", fn, waits, [("sP", 1)])
        return tok("sP")

    onec = arena[:, ar.regions["ones"][0]:ar.regions["ones"][0] + 1]
    oner = arena[0:1, ar.regions["oner"][0]:ar.regions["oner"][0] + 128]
    ident = arena[:, ar.regions["ident"][0]:ar.regions["ident"][0] + 128]

    def Sbf_ap(h):
        o = ar.regions["Sbf"][0]
        r = (h % 2) * 64
        c = (h // 2) * 64
        return arena[r:r + 64, o + c:o + c + 64]

    def Fsc_ap(h):
        o = ar.regions["Fsc"][0]
        r = (h % 2) * 64
        c = (h // 2) * 64
        return arena[r:r + 64, o + c:o + c + 64]

    def PSW_S(h):
        r = (h % 2) * 64
        c = (h // 2) * 64
        return PSW[r:r + 64, c:c + 64]

    dma("sync", ar.bf("qdm"), qdm_d.ap(), sem="dC")
    dma("sync", ar.bf("kdm"), kdm_d.ap(), sem="dC")
    dma("sync", ar.f32("bdm"), bdm_d.ap(), sem="dC")
    dma("sync", ar.f32("normw"), normw.ap(), sem="dC")
    dma("sync", arena[:, ar.regions["ones"][0]:ar.regions["ones"][0] + 8], onec_d.ap(), sem="dC")
    dma("sync", arena[0:1, ar.regions["oner"][0]:ar.regions["oner"][0] + 128], oner_d.ap(), sem="dC")
    dma("sync", arena[:, ar.regions["ident"][0]:ar.regions["ident"][0] + 128], ident_d.ap(), sem="dC")
    scst16 = arena[:, ar.regions["scst"][0]:ar.regions["scst"][0] + GU_MC * 16].bitcast(F16)
    t_ld = dma("sync", scst16.rearrange("p (c s) -> p c s", s=16),
               gusc.ap().rearrange("(c p) s -> p c s", p=128), sem="dC")
    t_scgu = cV(lambda e: e.tensor_copy(ar.f32("scgu"), scst16), [t_ld])
    scst16d = arena[:, ar.regions["scst"][0]:ar.regions["scst"][0] + DN_MC * 44].bitcast(F16)
    t_ld2 = dma("sync", scst16d.rearrange("p (c s) -> p c s", s=44),
                dnsc.ap().rearrange("(c p) s -> p c s", p=128), waits=[t_scgu], sem="dC")
    t_scdn = cV(lambda e: e.tensor_copy(ar.f32("scdn"), scst16d), [t_ld2])
    t_S0 = cV(lambda e: e.memset(ar.f32("S"), 0.0))
    t_Sbf = cP(lambda e: e.memset(ar.bf("Sbf"), 0.0))
    t_consts = tok("dC")

    gu_done = [None] * GU_MC
    dn_done = [None] * (DN_MC * 4)
    dq = {"pk_par": 0, "pk_war": [None, None], "v_rd": None,
          "vb_rd": None, "tr_dma": None}

    def _dq_nib_dve(pk, W, extra_waits):
        vi = ar.i32("dq_v", 8 * W)
        ts_ = [ar.i32("dq_t", W, i * 256) for i in range(6)]
        w0 = list(extra_waits)
        for j in range(8):
            def f(e, j=j, pk=pk, W=W, vi=vi, ts_=ts_):
                t, u, sb, m1, pp, g = ts_
                e.tensor_scalar(t[:], pk, 4 * j, None, op0=ALU.logical_shift_right)
                e.tensor_scalar(u[:], t[:], 7, None, op0=ALU.bitwise_and)
                e.tensor_scalar(sb[:], t[:], 8, 28, op0=ALU.bitwise_and, op1=ALU.logical_shift_left)
                e.tensor_scalar(m1[:], u[:], 1, None, op0=ALU.min)
                e.scalar_tensor_tensor(pp[:], u[:], 251, m1[:], op0=ALU.add, op1=ALU.mult)
                e.tensor_scalar(g[:], u[:], 2, None, op0=ALU.is_ge)
                e.tensor_tensor(pp[:], pp[:], g[:], op=ALU.add)
                e.tensor_scalar(pp[:], pp[:], 22, None, op0=ALU.logical_shift_left)
                return e.tensor_tensor(vi.rearrange("p (w k) -> p w k", k=8)[:, :, j],
                                       pp[:], sb[:], op=ALU.bitwise_or)
            last = cV(f, w0)
            w0 = []
        return last, "sV"

    def _dq_nib_pool(pk, W, extra_waits):
        vf = ar.i32("dq_v", 8 * W).bitcast(F32)
        n_i = ar.i32("dq_t", W, 0)
        nf = ar.f32("dq_t", W, 256)
        s = ar.f32("dq_t", W, 512)
        u = ar.f32("dq_t", W, 768)
        r4 = ar.f32("dq_t", W, 1024)
        r6 = ar.f32("dq_t", W, 1280)
        w0 = list(extra_waits)
        last = None
        for j in range(8):
            t_n = cV(lambda e, j=j: e.tensor_scalar(
                n_i[:], pk, 4 * j, 15, op0=ALU.logical_shift_right, op1=ALU.bitwise_and),
                w0 + ([last] if last else []))
            w0 = []
            def fp(e, j=j):
                e.tensor_copy(nf[:], n_i[:])
                e.tensor_scalar(s[:], nf[:], 8.0, None, op0=ALU.is_ge)
                e.tensor_scalar(u[:], s[:], -8.0, None, op0=ALU.mult)
                e.tensor_tensor(u[:], u[:], nf[:], op=ALU.add)
                e.tensor_scalar(r4[:], u[:], -4.0, None, op0=ALU.add)
                e.tensor_scalar(r6[:], r4[:], -2.0, None, op0=ALU.add)
                e.tensor_scalar(r4[:], r4[:], 0.0, None, op0=ALU.max)
                e.tensor_scalar(r6[:], r6[:], 0.0, None, op0=ALU.max)
                e.tensor_tensor(r4[:], r4[:], u[:], op=ALU.add)
                e.tensor_tensor(r6[:], r6[:], r6[:], op=ALU.add)
                e.tensor_tensor(r4[:], r4[:], r6[:], op=ALU.add)
                e.tensor_scalar(r4[:], r4[:], 0.5, None, op0=ALU.mult)
                e.tensor_scalar(s[:], s[:], -2.0, None, op0=ALU.mult)
                e.tensor_scalar(s[:], s[:], 1.0, None, op0=ALU.add)
                return e.tensor_tensor(vf.rearrange("p (w k) -> p w k", k=8)[:, :, j],
                                       r4[:], s[:], op=ALU.mult)
            last = cP(fp, [t_n])
        return last, "sP"

    def _dq_chunk(pk_src, W, scview, goff, ngroups, grsz, dst, use_pool):
        par = dq["pk_par"]
        dq["pk_par"] ^= 1
        pk = ar.i32("dq_pk", W, par * 256)
        wt = [w for w in (dq["pk_war"][par],) if w]
        tpk = dma("scalar", pk, pk_src, waits=wt, sem=f"dPK{par}")
        extra = [tpk]
        if dq["v_rd"]:
            extra.append(dq["v_rd"])
        if use_pool:
            t_nib, _ = _dq_nib_pool(pk, W, extra)
        else:
            t_nib, _ = _dq_nib_dve(pk, W, extra)
        dq["pk_war"][par] = t_nib
        wv = [t_nib] + ([dq["vb_rd"]] if dq["vb_rd"] else [])
        for g in range(ngroups):
            def fs(e, g=g):
                return e.tensor_scalar(
                    ar.bf("dq_vb", grsz, g * grsz),
                    ar.i32("dq_v", 8 * W).bitcast(F32)[:, g * grsz:(g + 1) * grsz],
                    scview[:, goff + g:goff + g + 1], None, op0=ALU.mult)
            t_sc = cV(fs, wv)
            wv = []
        dq["v_rd"] = t_sc
        nkc = ngroups
        t_out = None
        for r in range((nkc + 7) // 8):
            kcs = list(range(r * 8, min(nkc, r * 8 + 8)))
            for kc in kcs:
                wtt = []
                if kc == kcs[0]:
                    wtt = [t_sc] + ([dq["tr_dma"]] if dq["tr_dma"] else [])
                def ftr(e, kc=kc):
                    return e.dma_start(ar.bf("dq_tr", 128, (kc % 8) * 128),
                                       ar.bf("dq_vb", 128, kc * 128), transpose=True)
                p.do("scalar", ftr, wtt, [("dTR", 16)])
            t_tr = tok("dTR")
            n = len(kcs)
            t_out = dma("scalar", dst[:, r * 8:r * 8 + n, :],
                        ar.bf("dq_tr", n * 128).rearrange("p (k m) -> p k m", m=128),
                        waits=[t_tr], sem="dDQ")
            dq["tr_dma"] = t_out
        dq["vb_rd"] = tok("dTR")
        return t_out

    def plan_dequant_gu(mc, use_pool):
        dst = wgu.ap()[mc // 4, :, :, (mc % 4) * 128:(mc % 4) * 128 + 128]
        dst = dst.rearrange("k p m -> p k m")
        t = _dq_chunk(gup.ap()[mc * 128:(mc + 1) * 128, :], GU_W,
                      ar.f32("scgu", 16, mc * 16), 0, 16, 128, dst, use_pool)
        gu_done[mc] = t

    def plan_dequant_dn(qq, use_pool):
        M, quar = qq // 4, qq % 4
        dst = wdn.ap()[M, :, quar * 11:quar * 11 + 11, :]
        t = _dq_chunk(dnp.ap()[M * 128:(M + 1) * 128, quar * 176:(quar + 1) * 176], 176,
                      ar.f32("scdn", 44, M * 44), quar * 11, 11, 128, dst, use_pool)
        dn_done[qq] = t

    dq_queue = []
    gi = di = 0
    while gi < GU_MC or di < DN_MC * 4:
        if di * 88 >= gi * 64 and gi < GU_MC or di >= DN_MC * 4:
            dq_queue.append(("gu", gi)); gi += 1
        else:
            dq_queue.append(("dn", di)); di += 1
    dq_pos = [0]
    POOL_SHARE = 0.0

    def pump_dequant(n):
        for _ in range(n):
            if dq_pos[0] >= len(dq_queue):
                return
            kind, idx = dq_queue[dq_pos[0]]
            use_pool = (dq_pos[0] % 20) < int(20 * POOL_SHARE)
            (plan_dequant_gu if kind == "gu" else plan_dequant_dn)(idx, use_pool)
            dq_pos[0] += 1

    qkvv = ar.bf("qkv")
    xbv = ar.bf("xb")
    kpv = ar.bf("kp")
    vpv = ar.bf("vp")

    t_xr = [None, None]
    t_wf = [None, None]
    t_wbf = [None] * 16
    t_wbf_rd = [None] * 16
    t_xb = [None] * 16
    t_ps_evac = [None] * 4
    t_qkv_rd_prev = None
    t_kp = None
    t_vp = None
    t_Supd = t_S0
    t_awdd = [None, None]
    t_inter = [None, None]
    t_yev = [None] * 4
    t_ydma = [None] * 4
    t_qp = [None, None]
    t_ds_last = None
    t_scan_aw_rd = [None, None, None, None]
    t_ddm_slot = [None] * 4
    t_dddma = [None] * 4

    def dd_issue(pr):
        slot = pr % 4
        t_dddma[slot] = dma("sync", ar.bf("ddr", 1024, slot * 1024),
                            ddT_d.ap()[:, pr * 1024:(pr + 1) * 1024],
                            waits=[w for w in (t_ddm_slot[slot],) if w], sem=f"dDD{slot}")

    for pr0 in range(4):
        dd_issue(pr0)

    NCC = 1 if stop_after in ("p0c1", "qkv", "tr", "aw", "qst", "trnq", "blk1") else NC0
    for cc in range(NCC):
        t0 = cc * TC
        for kc in range(NCH):
            xsl = ar.f32("xr", 512, (kc % 2) * 512)
            td = dma("sync", xsl, x_fm.ap()[:, kc, t0:t0 + TC],
                     waits=[w for w in (t_xr[kc % 2],) if w], sem=f"dX{kc % 2}")
            wts = [td] + ([t_qkv_rd_prev] if (kc < 2 and t_qkv_rd_prev) else [])
            t_xb[kc] = cP(lambda e, kc=kc, xs=xsl: e.tensor_copy(
                xbv[:, kc * TC:(kc + 1) * TC], xs), wts)
            t_xr[kc % 2] = t_xb[kc]
        for ms in range(12):
            for kh in range(2):
                for kc8 in range(8):
                    kc = kh * 8 + kc8
                    wsl = ar.f32("wf32", 512, (kc8 % 2) * 512)
                    td = dma("sync", wsl, qkvT.ap()[kc * 128:(kc + 1) * 128, ms * 512:(ms + 1) * 512],
                             waits=[w for w in (t_wf[kc8 % 2],) if w], sem=f"dW{kc8 % 2}")
                    t_wbf[kc8] = cA(lambda e, kc8=kc8, w=wsl: e.copy(
                        ar.bf("wbf", 512, kc8 * 512), w),
                        [td] + ([t_wbf_rd[kc8]] if t_wbf_rd[kc8] else []))
                    t_wf[kc8 % 2] = t_wbf[kc8]
                for kc8 in range(8):
                    kc = kh * 8 + kc8
                    for msub in range(4):
                        wts = [t_wbf[kc8], t_xb[kc]]
                        if kc == 0:
                            if t_ps_evac[msub]:
                                wts.append(t_ps_evac[msub])
                            if cc > 0 and ms == 0:
                                wts += [w for w in t_yev + t_awdd if w]
                        def fmm(e, kc=kc, kc8=kc8, msub=msub):
                            return e.matmul(PS[msub][:],
                                            ar.bf("wbf", 128, kc8 * 512 + msub * 128),
                                            xbv[:, kc * TC:(kc + 1) * TC],
                                            start=(kc == 0), stop=(kc == 15))
                        t_mm = cT(fmm, [w for w in wts if w])
                    t_wbf_rd[kc8] = t_mm
            for msub in range(4):
                mc = ms * 4 + msub
                wse = [t_mm]
                if ms == 0 and msub == 0 and cc > 0:
                    wse.append(tok("dQS"))
                t_ps_evac[msub] = cA(lambda e, msub=msub, mc=mc: e.activation(
                    qkvv[:, mc * TC:(mc + 1) * TC], PS[msub][:], AF.Silu), wse)
        t_qkv_rd_prev = t_mm
        t_qkv_done = tok("sA")
        if stop_after == "qkv":
            break
        if stop_after != "trnq":
            for mc in range(NCH):
                dma("scalar", q_st.ap()[:, mc, t0:t0 + TC], qkvv[:, mc * TC:(mc + 1) * TC],
                    waits=[t_qkv_done], sem="dQS")
        if stop_after == "qst":
            break
        for bi in range(2):
            wts0 = [t_qkv_done, t_Supd]
            if t_ds_last:
                wts0.append(t_ds_last)
            if t_kp:
                wts0.append(t_kp)
            for jh in range(2):
                for pr in range(16):
                    c = (16 + pr) * TC + bi * 256 + jh * 128
                    p.do("sync", lambda e, c=c, jh=jh, pr=pr: e.dma_start(
                        kpv[:, jh * 2048 + pr * 128:jh * 2048 + (pr + 1) * 128],
                        qkvv[:, c:c + 128], transpose=True),
                        wts0 if (jh == 0 and pr == 0) else (), [("dK", 16)])
            t_ktr = tok("dK")
            for jh in range(2):
                t_kp = cV(lambda e, jh=jh: e.tensor_tensor(
                    kpv[:, jh * 2048:(jh + 1) * 2048], kpv[:, jh * 2048:(jh + 1) * 2048],
                    ar.bf("kdm", 2048, jh * 2048), op=ALU.mult), [t_ktr] if jh == 0 else ())
            for jh in range(2):
                for pr in range(16):
                    c = (32 + pr) * TC + bi * 256 + jh * 128
                    p.do("sync", lambda e, c=c, jh=jh, pr=pr: e.dma_start(
                        vpv[:, jh * 2048 + pr * 128:jh * 2048 + (pr + 1) * 128],
                        qkvv[:, c:c + 128], transpose=True), (), [("dV", 16)])
            t_vp = tok("dV")
            def plan_awT(pr):
                ring = pr % 2
                for hh in range(2):
                    psp = PS[1 + ring * 2 + hh]
                    for jh in range(2):
                        wts = []
                        if hh == 0 and jh == 0:
                            wts = [w for w in (t_vp, t_scan_aw_rd[ring * 2],
                                               t_scan_aw_rd[ring * 2 + 1]) if w]
                        def fa(e, pr=pr, hh=hh, jh=jh, psp=psp, bi=bi):
                            c = (16 + pr) * TC + bi * 256 + jh * 128
                            q0 = pr * TC + bi * 256
                            return e.matmul(psp[:, jh * 256:jh * 256 + 256],
                                            qkvv[hh * 64:hh * 64 + 64, c:c + 128],
                                            qkvv[hh * 64:hh * 64 + 64, q0:q0 + 256],
                                            start=True, stop=True)
                        t_aw = cT(fa, wts)
                t_qp[pr % 2] = cP(lambda e, pr=pr, bi=bi: e.tensor_tensor(
                    ar.bf("qp", 256, (pr % 2) * 256),
                    qkvv[:, pr * TC + bi * 256:pr * TC + bi * 256 + 256],
                    ar.bf("qdm", 256, pr * 256), op=ALU.mult),
                    [t_qkv_done] + ([t_inter[pr % 2]] if t_inter[pr % 2] else []))
                return t_aw

            def plan_dd(pr, t_aw):
                ring = pr % 2
                slot = pr % 4
                for hh in range(2):
                    psp = PS[1 + ring * 2 + hh]
                    wts = [t_aw]
                    if hh == 0:
                        wts.append(t_dddma[slot])
                        if t_inter[ring]:
                            wts.append(t_inter[ring])
                    t_dd = cV(lambda e, pr=pr, hh=hh, psp=psp, ring=ring, slot=slot:
                              e.tensor_tensor(
                                  ar.bf("awb", 512, ring * 1024 + hh * 512), psp[:],
                                  ar.bf("ddr", 512, slot * 1024 + hh * 512), op=ALU.mult), wts)
                    t_scan_aw_rd[ring * 2 + hh] = t_dd
                t_awdd[ring] = t_dd
                t_ddm_slot[slot] = t_dd
                dd_issue((pr + 4) % 16)
                return t_dd

            def plan_y_dS(pr, t_dd):
                ring = pr % 2
                ytensor = PS[0] if ring == 0 else PS[5]
                first_wts = [t_dd, t_kp, t_vp, t_qp[ring], t_Sbf]
                if t_yev[(pr - 2) % 4]:
                    first_wts.append(t_yev[(pr - 2) % 4])
                first = True
                for hh in range(2):
                    h = 2 * pr + hh
                    for jh in range(2):
                        def fi(e, hh=hh, jh=jh, h=h, ring=ring, yt=ytensor):
                            return e.matmul(yt[hh * 64:hh * 64 + 64, 0:256],
                                            vpv[:, jh * 2048 + h * 64:jh * 2048 + h * 64 + 64],
                                            ar.bf("awb", 256, ring * 1024 + hh * 512 + jh * 256),
                                            start=(jh == 0), stop=False)
                        cT(fi, [w for w in first_wts if w] if first else [])
                        first = False
                    def fint(e, hh=hh, h=h, yt=ytensor):
                        return e.matmul(yt[hh * 64:hh * 64 + 64, 0:256],
                                        Sbf_ap(h),
                                        ar.bf("qp", 256, (0 if yt is PS[0] else 1) * 256)[hh * 64:hh * 64 + 64, :],
                                        start=False, stop=True)
                    t_y = cT(fint)
                t_inter[ring] = t_y
                ysl = ar.bf("yr", 256, (pr % 4) * 256)
                t_ev = cA(lambda e, ysl=ysl, yt=ytensor: e.copy(ysl, yt[:, 0:256]),
                    [t_y] + ([t_ydma[pr % 4]] if t_ydma[pr % 4] else []))
                t_yev[pr % 4] = t_ev
                t_ydma[pr % 4] = dma("scalar",
                                     y_st.ap()[:, pr, t0 + bi * 256:t0 + bi * 256 + 256],
                                     ysl, waits=[t_ev], sem=f"dY{pr % 4}")
                for hh in range(2):
                    h = 2 * pr + hh
                    for jh in range(2):
                        def fds(e, h=h, jh=jh):
                            return e.matmul(PSW_S(h),
                                            kpv[:, jh * 2048 + h * 64:jh * 2048 + h * 64 + 64],
                                            vpv[:, jh * 2048 + h * 64:jh * 2048 + h * 64 + 64],
                                            start=(jh == 0), stop=(jh == 1))
                        t_ds = cT(fds, [t_y] if (hh == 0 and jh == 0) else [])
                return t_ds

            if stop_after == "aw":
                for pr in range(16):
                    plan_dd(pr, plan_awT(pr))
                continue
            t_aw_p = plan_awT(0)
            t_dd_p = plan_dd(0, t_aw_p)
            for pr in range(1, 16):
                t_aw_n = plan_awT(pr)
                t_ds_last = plan_y_dS(pr - 1, t_dd_p)
                t_dd_p = plan_dd(pr, t_aw_n)
            t_ds_last = plan_y_dS(15, t_dd_p)
            cV(lambda e: e.tensor_tensor(ar.f32("S"), ar.f32("S"), ar.f32("bdm"), op=ALU.mult),
               [t_ds_last])
            t_Supd = cV(lambda e: e.tensor_tensor(ar.f32("S"), ar.f32("S"), PSW[:], op=ALU.add))
            t_Sbf = cP(lambda e: e.tensor_copy(ar.bf("Sbf"), ar.f32("S")), [t_Supd])
            if cc == 0 and bi == 0 and dbg:
                if "kp" in dbg:
                    dma("scalar", dbg["kp"].ap(), kpv[:], waits=[t_kp])
                if "vp" in dbg:
                    dma("scalar", dbg["vp"].ap(), vpv[:], waits=[t_vp])
                if "qp" in dbg:
                    dma("scalar", dbg["qp"].ap(), ar.bf("qp"), waits=[t_qp[0], t_qp[1]])
                if "S1" in dbg:
                    dma("scalar", dbg["S1"].ap(), ar.f32("S"), waits=[t_Supd])
            if stop_after == "blk1":
                break
        if dq_on:
            pump_dequant(19)
    if dq_on:
        pump_dequant(10 ** 6)

    if stop_after in ("p0c1", "qkv", "tr", "aw", "qst", "trnq", "blk1"):
        dbg_w0 = [tok(s) for s in sem_names if s.startswith("d")]
        if "y" in dbg:
            dma("scalar", dbg["y"].ap(), y_st.ap(), waits=dbg_w0)
        if "q" in dbg:
            dma("scalar", dbg["q"].ap(), q_st.ap(), waits=dbg_w0)
        final_waits = [tok(s) for s in sem_names if s.startswith("d")]
        p.do("vector", lambda e: e.engine_nop(), final_waits, ())
        with nc.Block() as block:
            block.sync(lambda e: p.emit("sync", e, sems))
            block.scalar(lambda e: p.emit("scalar", e, sems))
            block.vector(lambda e: p.emit("vector", e, sems))
            block.tensor(lambda e: p.emit("tensor", e, sems))
            block.gpsimd(lambda e: p.emit("gpsimd", e, sems))
        es.close()
        return nc
    t_fin = dma("scalar", fin.ap(), ar.f32("S"), waits=[t_Supd], sem="dC")
    p.do("gpsimd", lambda e: e.collective_compute(
        "AllGather", ALU.bypass, replica_groups=[[0, 1], [2, 3], [4, 5], [6, 7]],
        ins=[fin.ap().opt()], outs=[fgo.ap().opt()]), [t_fin], [("sC", 1)])
    t_coll = tok("sC")
    if "F" in dbg:
        dma("scalar", dbg["F"].ap(), ar.f32("S"), waits=[t_Supd])

    t_F = dma("sync", ar.f32("cmb", 1024), fgo.ap()[0:128, :], waits=[t_coll], sem="dC")
    t_Fbf = cP(lambda e: e.tensor_copy(ar.bf("Fbf"), ar.f32("cmb", 1024)), [t_F])

    slope, _, _, _, _ = _decay_tables()
    Avals = np.exp(-slope[:, None] * (512.0 * np.arange(NT1)[None, :])).astype(np.float32)

    t_qr = [None] * 4
    t_zr = [None, None]
    t_sqw = [None, None]
    t_yfw = [None] * 16
    t_corr = [None, None]
    t_gw = [None, None]
    t_dnw = [None, None]
    t_hs = [None, None]
    t_ybw = [None, None]
    t_watc = [None] * 16
    t_watr = [None] * 16
    t_wfat = [None, None]
    t_psw_user = None
    t_prev_out = None
    t_prev_x32_rd = None

    def newton(stat_ap, scale, eps, waits):
        v = rows_f32(0)
        y = rows_f32(1)
        t1 = rows_f32(2)
        cV(lambda e: e.tensor_scalar(v, stat_ap, scale, eps, op0=ALU.mult, op1=ALU.add), waits)
        cV(lambda e: e.tensor_scalar(y.bitcast(I32), v.bitcast(I32), 1, None,
                                     op0=ALU.logical_shift_right))
        cV(lambda e: e.tensor_scalar(y.bitcast(I32), y.bitcast(I32), -1, 0x5F3759DF,
                                     op0=ALU.mult, op1=ALU.add))
        tk = None
        for _ in range(3):
            cV(lambda e: e.tensor_tensor(t1, y, y, op=ALU.mult))
            cV(lambda e: e.tensor_tensor(t1, t1, v, op=ALU.mult))
            cV(lambda e: e.tensor_scalar(t1, t1, -0.5, 1.5, op0=ALU.mult, op1=ALU.add))
            tk = cV(lambda e: e.tensor_tensor(y, y, t1, op=ALU.mult))
        return tk, y

    def bcast(vals, waits, slot0=0):
        nonlocal t_psw_user
        n = len(vals)
        assert n == 1
        t_rhs = None
        for i, v in enumerate(vals):
            hi = rhs_bf(0, 512)
            lo = rhs_bf(512, 512)
            lov = rows_f32(2)
            cV(lambda e, v=v, hi=hi: e.tensor_copy(hi, v), waits if i == 0 else [])
            cV(lambda e, v=v, hi=hi, lov=lov: e.tensor_tensor(lov, v, hi, op=ALU.subtract))
            t_rhs = cV(lambda e, lo=lo, lov=lov: e.tensor_copy(lo, lov))
        outs = []
        t_last = None
        for i in range(n):
            for half in range(2):
                src = rhs_bf(half * 512, 512)
                wts = []
                if half == 0:
                    wts = ([t_rhs] if i == 0 else []) + ([t_psw_user] if t_psw_user else [])
                t_b = cT(lambda e, src=src, half=half: e.matmul(
                    PSW[:, half * 512:half * 512 + 512], oner, src, start=True, stop=True), wts)
            t_ev = cA(lambda e: e.copy(ar.f32("mrb", 1024), PSW[:]),
                      [t_b] + ([t_last] if t_last else []))
            dst = ar.f32("cmb", 512, slot0 * 512)
            t_last = cV(lambda e, dst=dst: e.tensor_tensor(
                dst, ar.f32("mrb", 512, 0), ar.f32("mrb", 512, 512), op=ALU.add), [t_ev])
            outs.append(dst)
            t_psw_user = t_ev
        return t_last, outs

    t_prev_down = None
    NTT = {"p0": 0, "p1t1": 1}.get(stop_after, NT1)
    for tl in range(NTT):
        t0 = tl * TT
        guard = [w for w in (t_prev_out, t_prev_down) if w]
        wx = list(guard) + ([t_prev_x32_rd] if t_prev_x32_rd else [])
        if tl == 0:
            wx = [t_fin, tok("sV"), tok("sT"), tok("sA"), tok("sP")]
        t_xd = dma("sync", ar.f32("x32").rearrange("p (c t) -> p c t", t=TT),
                   x_fm.ap()[:, :, t0:t0 + TT], waits=wx, sem="dX0")
        t_xbc = [None] * NCH
        for mc in range(NCH):
            wts = [t_xd]
            if mc == 0:
                wts += guard + ([tok("sT")] if tl > 0 else [])
            t_xbc[mc] = cP(lambda e, mc=mc: e.tensor_copy(
                ar.bf("xb1", 512, mc * 512), ar.f32("x32", 512, mc * 512)), wts)
        wq5 = list(guard)
        if tl == 0:
            wq5 = [tok("sT"), tok("sV"), tok("sA")]
        t_qd512 = dma("sync", ar.bf("qd512"), qd512_d.ap(), waits=wq5, sem="dQ5")
        t_Fsc = None
        for hg in range(2):
            for hc in range(16):
                hdx = 2 * hc + hg
                a = float(Avals[hdx, tl])
                wts = [t_Fbf] if (hg == 0 and hc == 0) else []
                if hg == 0 and hc == 0 and tl > 0:
                    wts.append(tok("sT"))
                t_Fsc = cP(lambda e, hg=hg, hc=hc, a=a: e.tensor_scalar(
                    Fsc_ap(2 * hc + hg), arena[hg * 64:hg * 64 + 64,
                                               ar.regions["Fbf"][0] + hc * 64:
                                               ar.regions["Fbf"][0] + hc * 64 + 64],
                    a, None, op0=ALU.mult), wts)
        t_statq = None
        for pr in range(16):
            wy = [w for w in (t_yfw[pr],) if w] + (guard if pr == 0 else [])
            if tl == 0 and pr == 0:
                wy += [tok("dY0"), tok("dY1"), tok("dY2"), tok("dY3")]
            dma("scalar", ar.bf("yf", 512, pr * 512), y_st.ap()[:, pr, t0:t0 + TT],
                waits=wy, sem="dYL")
        t_yall = tok("dYL")
        for pr in range(16):
            qsl = ar.bf("qr", 512, (pr % 4) * 512)
            wq = [w for w in (t_qr[pr % 4],) if w] + (guard if pr == 0 else [])
            if tl == 0 and pr == 0:
                wq.append(tok("dQS"))
            tq = dma("scalar", qsl, q_st.ap()[:, pr, t0:t0 + TT],
                     waits=wq, sem=f"dQ{pr % 4}")
            ysl = ar.bf("yf", 512, pr * 512)
            ty = t_yall
            zsl = ar.bf("zr", 512, (pr % 2) * 512)
            t_z = cP(lambda e, q=qsl, z=zsl, pr=pr: e.tensor_tensor(
                z, q, ar.bf("qd512", 512, pr * 512), op=ALU.mult),
                [tq, t_Fsc, t_qd512] + ([t_zr[pr % 2]] if t_zr[pr % 2] else [])
                + (guard if pr < 2 else []))
            t_zr[pr % 2] = t_z
            t_qr[pr % 4] = t_z
            cps = PS[3 + pr % 2]
            for hh in range(2):
                h = 2 * pr + hh
                wts = [t_z]
                if hh == 0:
                    if t_corr[pr % 2]:
                        wts.append(t_corr[pr % 2])
                    if pr < 2:
                        wts += guard
                    if tl == 0 and pr < 2:
                        wts += [w for w in t_awdd if w]
                t_cm = cT(lambda e, hh=hh, h=h, z=zsl, cps=cps: e.matmul(
                    cps[hh * 64:hh * 64 + 64, :], Fsc_ap(h),
                    z[hh * 64:hh * 64 + 64, :], start=True, stop=True), wts)
            t_yfull = cV(lambda e, y=ysl, cps=cps: e.tensor_tensor(y, y, cps[:], op=ALU.add),
                         [t_cm, ty])
            t_corr[pr % 2] = t_yfull
            t_yfw[pr] = t_yfull
            sqs = ar.bf("sqr", 512, (pr % 2) * 512)
            t_sq = cP(lambda e, y=ysl, s=sqs: e.tensor_tensor(s, y, y, op=ALU.mult),
                      [t_yfull] + ([t_sqw[pr % 2]] if t_sqw[pr % 2] else [])
                      + (guard if pr < 2 else []))
            wts = [t_sq]
            if pr == 0:
                wts.append(t_Supd)
                if t_psw_user:
                    wts.append(t_psw_user)
            t_statq = cT(lambda e, s=sqs, pr=pr: e.matmul(
                PSW[0:1, 0:512], onec, s, start=(pr == 0), stop=(pr == 15)), wts)
            t_sqw[pr % 2] = t_statq
        wts_n = [t_statq] + ([t_Fbf] if tl == 0 else [])
        t_n, rsd = newton(PSW[0:1, 0:512], 1.0 / DM, EPS_ATTN, wts_n)
        t_cmby, (rby,) = bcast([rsd], [t_n])
        gp_done = [None] * 16
        t_g1 = [None, None]
        t_bank_rd = [None] * 4
        for sl in range(4):
            for kh in range(2):
                for kc8 in range(8):
                    kc = kh * 8 + kc8
                    wsl = ar.f32("wat", 512, 2048 + (kc8 % 2) * 512)
                    wdd = [w for w in (t_wfat[kc8 % 2],) if w]
                    if sl == 0 and kh == 0 and kc8 < 2:
                        wdd += guard
                        if tl == 0:
                            wdd.append(tok("sA"))
                    td = dma("sync", wsl, gateT.ap()[kc * 128:(kc + 1) * 128, sl * 512:(sl + 1) * 512],
                             waits=wdd, sem=f"dW{kc8 % 2}")
                    wts = [td] + ([t_watr[kc8]] if t_watr[kc8] else [])
                    if sl == 0 and kh == 0 and kc8 < 2:
                        wts += guard
                    t_watc[kc8] = cA(lambda e, kc8=kc8, w=wsl: e.copy(
                        ar.bf("wat", 512, kc8 * 512), w), wts)
                    t_wfat[kc8 % 2] = t_watc[kc8]
                    for mcs in range(4):
                        mc = sl * 4 + mcs
                        wts = [t_watc[kc8], t_xbc[kc]]
                        if kc == 0 and sl > 0 and t_bank_rd[mcs]:
                            wts.append(t_bank_rd[mcs])
                        if kc == 0 and sl == 0:
                            if mcs == 3:
                                wts += [w for w in t_corr if w]
                            if tl == 0:
                                wts += [w for w in t_yev if w]
                                wts += [w for w in t_ps_evac if w]
                                wts += [w for w in t_awdd if w]
                        t_gmm = cT(lambda e, kc=kc, kc8=kc8, mcs=mcs, mc=mc: e.matmul(
                            PS[mcs][:], ar.bf("wat", 128, kc8 * 512 + mcs * 128),
                            ar.bf("xb1", 512, kc * 512), start=(kc == 0), stop=(kc == 15)),
                            [w for w in wts if w])
                    t_watr[kc8] = t_gmm
            for mcs in range(4):
                mc = sl * 4 + mcs
                g1s = ar.f32("g1", 512, (mc % 2) * 512)
                t_t = cA(lambda e, g=g1s, mcs=mcs: e.activation(g, PS[mcs][:], AF.Tanh, scale=0.5),
                         [t_gmm] + ([t_g1[mc % 2]] if t_g1[mc % 2] else [])
                         + (guard if mc < 2 else []))
                cV(lambda e, g=g1s: e.tensor_scalar(g, g, 0.5, 0.5, op0=ALU.mult, op1=ALU.add),
                   [t_t, t_cmby])
                t_g1[mc % 2] = cV(lambda e, mc=mc, g=g1s, rby=rby: e.tensor_tensor(
                    ar.bf("gp", 512, mc * 512), g, rby, op=ALU.mult),
                   (guard if mc < 2 else []))
                gp_done[mc] = t_g1[mc % 2]
                t_bank_rd[mcs] = t_t
        t_ag = [None] * 16
        for pr in range(16):
            t_ag[pr] = cV(lambda e, pr=pr: e.scalar_tensor_tensor(
                ar.bf("ag", 512, pr * 512), ar.bf("yf", 512, pr * 512),
                ar.f32("normw", 1, pr), ar.bf("gp", 512, pr * 512),
                op0=ALU.mult, op1=ALU.mult),
                [gp_done[pr], t_yfw[pr]] + (guard if pr == 0 else []))
        z1_done = [None] * 16
        for sl in range(4):
            for kh in range(2):
                for kc8 in range(8):
                    kc = kh * 8 + kc8
                    wsl = ar.f32("wat", 512, 2048 + (kc8 % 2) * 512)
                    td = dma("sync", wsl, outT.ap()[kc * 128:(kc + 1) * 128, sl * 512:(sl + 1) * 512],
                             waits=[w for w in (t_wfat[kc8 % 2],) if w], sem=f"dW{kc8 % 2}")
                    t_watc[kc8] = cA(lambda e, kc8=kc8, w=wsl: e.copy(
                        ar.bf("wat", 512, kc8 * 512), w),
                        [td] + ([t_watr[kc8]] if t_watr[kc8] else []))
                    t_wfat[kc8 % 2] = t_watc[kc8]
                    for mcs in range(4):
                        mc = sl * 4 + mcs
                        wts = [t_watc[kc8], t_ag[kc]]
                        if kc == 0 and t_bank_rd[mcs]:
                            wts.append(t_bank_rd[mcs])
                        t_omm = cT(lambda e, kc=kc, kc8=kc8, mcs=mcs, mc=mc: e.matmul(
                            PS[mcs][:], ar.bf("wat", 128, kc8 * 512 + mcs * 128),
                            ar.bf("ag", 512, kc * 512), start=(kc == 0), stop=(kc == 15)),
                            [w for w in wts if w])
                    t_watr[kc8] = t_omm
            for mcs in range(4):
                mc = sl * 4 + mcs
                z1_done[mc] = cV(lambda e, mc=mc, mcs=mcs: e.tensor_tensor(
                    ar.f32("x32", 512, mc * 512), ar.f32("x32", 512, mc * 512),
                    PS[mcs][:], op=ALU.add), [t_omm])
                t_bank_rd[mcs] = z1_done[mc]
        t_statmm = None
        for mc in range(NCH):
            sl2 = mc % 2
            t_zc = cA(lambda e, mc=mc, sl2=sl2: e.copy(
                ar.bf("hs", 512, sl2 * 1024), ar.f32("x32", 512, mc * 512)),
                [z1_done[mc]] + ([t_statmm] if mc >= 2 else ([t_psw_user] if t_psw_user else [])))
            t_zs = cP(lambda e, sl2=sl2: e.tensor_tensor(
                ar.bf("hs", 512, sl2 * 1024 + 512), ar.bf("hs", 512, sl2 * 1024),
                ar.bf("hs", 512, sl2 * 1024), op=ALU.mult), [t_zc])
            wts = [t_zs]
            if mc == 0 and t_psw_user:
                wts.append(t_psw_user)
            cT(lambda e, mc=mc, sl2=sl2: e.matmul(
                PSW[0:1, 0:512], onec, ar.bf("hs", 512, sl2 * 1024),
                start=(mc == 0), stop=(mc == NCH - 1)), wts)
            t_statmm = cT(lambda e, mc=mc, sl2=sl2: e.matmul(
                PSW[0:1, 512:1024], onec, ar.bf("hs", 512, sl2 * 1024 + 512),
                start=(mc == 0), stop=(mc == NCH - 1)))
        m_sv = ar.f32("mrb", 512, 0)[0:1, :]
        cV(lambda e: e.tensor_scalar(m_sv, PSW[0:1, 0:512], 1.0 / DM, None, op0=ALU.mult),
           [t_statmm])
        vv = rows_f32(0)
        cV(lambda e: e.tensor_scalar(vv, PSW[0:1, 512:1024], 1.0 / DM, None, op0=ALU.mult))
        t2 = rows_f32(2)
        cV(lambda e: e.tensor_tensor(t2, m_sv, m_sv, op=ALU.mult))
        t_var = cV(lambda e: e.tensor_tensor(vv, vv, t2, op=ALU.subtract))
        t_n1, rs1 = newton(vv, 1.0, EPS_LN, [t_var])
        t_cmbm, (mb1,) = bcast([m_sv], [t_n1], slot0=0)
        t_cmb1, (rb1,) = bcast([rs1], [t_cmbm], slot0=1)
        t_x1b = [None] * 16
        for mc in range(NCH):
            cV(lambda e, mc=mc, mb=mb1: e.tensor_tensor(
                ar.f32("x32", 512, mc * 512), ar.f32("x32", 512, mc * 512), mb, op=ALU.subtract),
                [t_cmb1] if mc == 0 else [])
            t_apq = cV(lambda e, mc=mc, rb=rb1: e.tensor_tensor(
                ar.f32("x32", 512, mc * 512), ar.f32("x32", 512, mc * 512), rb, op=ALU.mult))
            t_x1b[mc] = cP(lambda e, mc=mc: e.tensor_copy(
                ar.bf("xb1", 512, mc * 512), ar.f32("x32", 512, mc * 512)),
                [t_apq] + ([t_watr[15], tok("sT")] if mc == 0 else []))
        h_done = [None] * 44
        t_sil_prev = None
        for sl in range(44):
            twchunk = [None, None]
            for gu_i in range(2):
                mcw = 2 * sl + gu_i
                gsl = ar.bf("gw", 2048, gu_i * 2048)
                wts = [w for w in (t_gw[gu_i], gu_done[mcw]) if w]
                if sl == 0:
                    wts += [w for w in t_watr if w]
                    wts.append(tok("sA"))
                twchunk[gu_i] = dma(
                    "sync", gsl.rearrange("p (k m) -> p k m", m=128),
                    wgu.ap()[mcw // 4, :, :, (mcw % 4) * 128:(mcw % 4) * 128 + 128]
                    .rearrange("k p m -> p k m"), waits=wts, sem=f"dGW{gu_i}")
            for gu_i in range(2):
                ps_t = PS[gu_i]
                gsl = ar.bf("gw", 2048, gu_i * 2048)
                for kc in range(16):
                    wts = [twchunk[gu_i]]
                    if kc == 0:
                        wts.append(t_x1b[15])
                        if gu_i == 0 and t_sil_prev:
                            wts.append(t_sil_prev)
                        if gu_i == 1 and h_done[sl - 1]:
                            wts.append(h_done[sl - 1])
                        if sl == 0:
                            wts += [w for w in (t_bank_rd[0], t_bank_rd[1]) if w]
                    t_gu = cT(lambda e, kc=kc, ps_t=ps_t, gsl=gsl: e.matmul(
                        ps_t[:], gsl[:, kc * 128:kc * 128 + 128],
                        ar.bf("xb1", 512, kc * 512), start=(kc == 0), stop=(kc == 15)),
                        [w for w in wts if w])
                t_gw[gu_i] = t_gu
            hss = ar.bf("hs", 512, (sl % 2) * 1024)
            t_sil = cA(lambda e, h=hss: e.activation(h, PS[0][:], AF.Silu),
                       [t_gu] + ([t_hs[sl % 2]] if t_hs[sl % 2] else []))
            t_hs[sl % 2] = t_sil
            t_sil_prev = t_sil
            h_done[sl] = cV(lambda e, sl=sl, h=hss: e.tensor_tensor(
                hslot(sl), h, PS[1][:], op=ALU.mult),
                [t_sil] + (guard if sl < 2 else []))
        y32_done = [None] * 16
        t_stat2mm = None
        for M in range(DN_MC):
            t_dmm = None
            for half in range(2):
                slot = (M * 2 + half) % 2
                dsl = ar.bf("dnw", 2816, slot * 2816)
                wts = [w for w in (t_dnw[slot], dn_done[4 * M + 2 * half + 1]) if w]
                if M == 0 and half == 0:
                    wts += guard + [tok("sV")]
                tdn = dma("sync", dsl.rearrange("p (c m) -> p c m", m=128),
                          wdn.ap()[M, :, half * 22:(half + 1) * 22, :], waits=wts, sem=f"dDN{slot}")
                for ci in range(22):
                    c = half * 22 + ci
                    wts = [tdn, h_done[c]]
                    if c == 0:
                        if M >= 2 and y32_done[M - 2]:
                            wts.append(y32_done[M - 2])
                        if M < 2:
                            wts += [w for w in t_hs if w]
                            wts.append(h_done[43])
                    t_dmm = cT(lambda e, c=c, ci=ci, d=dsl, M=M: e.matmul(
                        PS[M % 2][:], d[:, ci * 128:ci * 128 + 128], hslot(c),
                        start=(c == 0), stop=(c == 43)), [w for w in wts if w])
                t_dnw[slot] = t_dmm
            t_ye = cA(lambda e, M=M: e.copy(y32slot(M), PS[M % 2][:]),
                      [t_dmm] + (guard if M == 0 else []))
            y32_done[M] = t_ye
            ybs = ar.bf("ybr", 512, (M % 2) * 512)
            cP(lambda e, M=M, y=ybs: e.tensor_copy(y, y32slot(M)),
               [t_ye] + ([t_ybw[M % 2]] if t_ybw[M % 2] else [])
               + (guard if M < 2 else []))
            t_yb = cP(lambda e, y=ybs: e.tensor_tensor(y, y, y, op=ALU.mult))
            wts = [t_yb]
            if M == 0 and t_psw_user:
                wts.append(t_psw_user)
            t_stat2mm = cT(lambda e, y=ybs, M=M: e.matmul(
                PSW[0:1, 0:512], onec, y, start=(M == 0), stop=(M == DN_MC - 1)), wts)
            t_ybw[M % 2] = t_stat2mm
        t_n2, rs2 = newton(PSW[0:1, 0:512], 1.0 / DM, EPS_FFN, [t_stat2mm])
        t_cmb2, (r2b,) = bcast([rs2], [t_n2])
        z2_done = [None] * 16
        for M in range(NCH):
            cV(lambda e, M=M, rb=r2b: e.tensor_tensor(y32slot(M), y32slot(M), rb, op=ALU.mult),
               [t_cmb2, y32_done[M]] if M == 0 else [y32_done[M]])
            z2_done[M] = cV(lambda e, M=M: e.tensor_tensor(
                y32slot(M), y32slot(M), ar.f32("x32", 512, M * 512), op=ALU.add))
        t_statmm = None
        for mc in range(NCH):
            sl2 = mc % 2
            t_zc = cA(lambda e, mc=mc, sl2=sl2: e.copy(
                ar.bf("hs", 512, sl2 * 1024), y32slot(mc)),
                [z2_done[mc]] + ([t_statmm] if mc >= 2 else [t_psw_user]))
            t_zs = cP(lambda e, sl2=sl2: e.tensor_tensor(
                ar.bf("hs", 512, sl2 * 1024 + 512), ar.bf("hs", 512, sl2 * 1024),
                ar.bf("hs", 512, sl2 * 1024), op=ALU.mult), [t_zc])
            wts = [t_zs]
            if mc == 0:
                wts.append(t_psw_user)
            cT(lambda e, mc=mc, sl2=sl2: e.matmul(
                PSW[0:1, 0:512], onec, ar.bf("hs", 512, sl2 * 1024),
                start=(mc == 0), stop=(mc == NCH - 1)), wts)
            t_statmm = cT(lambda e, mc=mc, sl2=sl2: e.matmul(
                PSW[0:1, 512:1024], onec, ar.bf("hs", 512, sl2 * 1024 + 512),
                start=(mc == 0), stop=(mc == NCH - 1)))
        m_sv = ar.f32("mrb", 512, 0)[0:1, :]
        cV(lambda e: e.tensor_scalar(m_sv, PSW[0:1, 0:512], 1.0 / DM, None, op0=ALU.mult),
           [t_statmm])
        vv = rows_f32(0)
        cV(lambda e: e.tensor_scalar(vv, PSW[0:1, 512:1024], 1.0 / DM, None, op0=ALU.mult))
        t2 = rows_f32(2)
        cV(lambda e: e.tensor_tensor(t2, m_sv, m_sv, op=ALU.mult))
        t_var = cV(lambda e: e.tensor_tensor(vv, vv, t2, op=ALU.subtract))
        t_n3, rs3 = newton(vv, 1.0, EPS_LN, [t_var])
        t_cmbm3, (mb3,) = bcast([m_sv], [t_n3], slot0=0)
        t_cmb3, (rb3,) = bcast([rs3], [t_cmbm3], slot0=1)
        t_last_out = None
        t_last_x32rd = None
        for M in range(NCH):
            cV(lambda e, M=M, mb=mb3: e.tensor_tensor(
                y32slot(M), y32slot(M), mb, op=ALU.subtract), [t_cmb3] if M == 0 else [])
            t_o = cV(lambda e, M=M, rb=rb3: e.tensor_tensor(
                ar.f32("x32", 512, M * 512), y32slot(M), rb, op=ALU.mult))
            t_last_x32rd = t_o
            t_last_out = dma("scalar", yo.ap()[:, M, t0:t0 + TT],
                             ar.f32("x32", 512, M * 512), waits=[t_o], sem="dOUT")
        t_prev_out = t_last_out
        t_prev_x32_rd = t_last_x32rd
        t_prev_down = t_dnw[1] or t_dnw[0]

    dbg_w = [tok(s) for s in sem_names if s.startswith("d")]
    for name in dbg:
        if name == "y":
            dma("scalar", dbg["y"].ap(), y_st.ap(), waits=dbg_w)
        elif name == "q":
            dma("scalar", dbg["q"].ap(), q_st.ap(), waits=dbg_w)
        elif name == "wgu":
            dma("scalar", dbg["wgu"].ap(), wgu.ap(), waits=dbg_w)
        elif name == "wdn":
            dma("scalar", dbg["wdn"].ap(), wdn.ap(), waits=dbg_w)

    final_waits = [tok(s) for s in sem_names if s.startswith("d")]
    p.do("vector", lambda e: e.engine_nop(), final_waits, ())

    with nc.Block() as block:
        block.sync(lambda e: p.emit("sync", e, sems))
        block.scalar(lambda e: p.emit("scalar", e, sems))
        block.vector(lambda e: p.emit("vector", e, sems))
        block.tensor(lambda e: p.emit("tensor", e, sems))
        block.gpsimd(lambda e: p.emit("gpsimd", e, sems))
    es.close()
    return nc



def _host_prep(inputs):
    x = np.asarray(inputs["L0_attn_hidden_states"])
    qkv_w = np.asarray(inputs["L0_attn_qkv_w"])
    out_w = np.asarray(inputs["L0_attn_out_w"])
    gate_w = np.asarray(inputs["L0_attn_gate_w"])
    norm_w = np.asarray(inputs["L0_attn_norm_w"])
    gupk = np.asarray(inputs["L0_ffn_gate_up_packed"])
    gusc = np.asarray(inputs["L0_ffn_gate_up_scales"])
    dnp_ = np.asarray(inputs["L0_ffn_down_packed"])
    dnsc = np.asarray(inputs["L0_ffn_down_scales"])

    hidx = np.arange(H)[:, None] * 192 + np.arange(64)[None, :]
    order = np.concatenate([(hidx + g * 64).reshape(-1) for g in range(3)])
    qkvT = np.ascontiguousarray(qkv_w[order].T)
    gateT = np.ascontiguousarray(gate_w.T)
    outT = np.ascontiguousarray(out_w.T)
    normw = np.ascontiguousarray(norm_w.reshape(NCH, 128).T)

    ridx = np.empty(2 * ID_, np.int64)
    for c in range(44):
        ridx[(2 * c) * 128:(2 * c) * 128 + 128] = np.arange(c * 128, c * 128 + 128)
        ridx[(2 * c + 1) * 128:(2 * c + 1) * 128 + 128] = np.arange(ID_ + c * 128, ID_ + c * 128 + 128)
    gupr = np.ascontiguousarray(gupk[ridx])
    guscT = np.ascontiguousarray(gusc.T[ridx])
    dnscT = np.ascontiguousarray(dnsc.T)

    onec = np.ones((128, 8), BF)
    oner = np.ones((1, 128), BF)
    ident = np.eye(128, dtype=BF)

    in_maps = []
    for c in range(8):
        b, half = c // 2, c % 2
        xh = x[b, half * T:(half + 1) * T, :]
        x_fmh = np.ascontiguousarray(xh.T.reshape(NCH, 128, T).transpose(1, 0, 2))
        consts = _build_consts(1.0 if half == 1 else 0.0)
        in_maps.append(dict(
            x_fm=x_fmh, qkvT=qkvT, gateT=gateT, outT=outT, normw=normw,
            gup=gupr, gusc=guscT, dnp=dnp_, dnsc=dnscT,
            ddT=consts["ddT"], qdm=consts["qdm"], kdm=consts["kdm"],
            bdm=consts["bdm"], qd512=consts["qd512"],
            onec=onec, oner=oner, ident=ident,
        ))
    return in_maps


def kernel(**inputs):
    in_maps = _host_prep(inputs)
    nc = build_nc()
    res = run_bass_kernel_spmd(nc, in_maps, core_ids=list(range(8)))
    out = np.empty((4, 2 * T, DM), np.float32)
    for c in range(8):
        b, half = c // 2, c % 2
        yov = np.asarray(res.results[c]["yo"])
        out[b, half * T:(half + 1) * T, :] = yov.transpose(2, 1, 0).reshape(T, DM)
    return out


# version 05
# speedup vs baseline: 2.1663x; 1.0617x over previous
import numpy as np
import ml_dtypes
from contextlib import ExitStack

import concourse.bass as bass
import concourse.mybir as mybir
from concourse.bass_utils import run_bass_kernel_spmd

F32 = mybir.dt.float32
BF16 = mybir.dt.bfloat16
F16 = mybir.dt.float16
I32 = mybir.dt.int32
AF = mybir.ActivationFunctionType
ALU = mybir.AluOpType
BF = ml_dtypes.bfloat16

H, D, BS, DM = 32, 64, 256, 2048
T = 4096
NBLK = T // BS
NCH = DM // 128
TC = 512
NC0 = T // TC
TT = 512
NT1 = T // TT
GU_MC = 88
GU_W = 256
DN_MC = 16
DN_W = 704
ID_ = 5632
EPS_ATTN = 1e-5
EPS_FFN = 1e-6
EPS_LN = 1e-5


def _decay_tables():
    h = np.arange(H, dtype=np.float64)
    slope = (0.5 ** (0.25 * (h + 1.0))) * (1.0 - 0.0 / (31 + 1e-5) + 1e-5)
    i = np.arange(BS, dtype=np.float64)
    qd = np.exp(-slope[:, None] * (i[None, :] + 1.0))
    kd = np.exp(-slope[:, None] * (BS - (i[None, :] + 1.0)))
    bd = np.exp(-slope * BS)
    dif = i[:, None] - i[None, :]
    dd = np.where(dif >= 0, np.exp(-slope[:, None, None] * dif[None]), 0.0)
    return slope, qd, kd, bd, dd


def _build_consts(mask_c):
    slope, qd, kd, bd, dd = _decay_tables()
    ddT = np.zeros((128, H * 2 * 256), np.float32)
    for pr in range(16):
        for hh in range(2):
            hd = 2 * pr + hh
            for jh in range(2):
                c0 = ((pr * 2 + hh) * 2 + jh) * 256
                ddT[:, c0:c0 + 256] = dd[hd, :, jh * 128:(jh + 1) * 128].T
    qdm = np.zeros((128, 16 * 256), np.float32)
    for pr in range(16):
        qdm[0:64, pr * 256:(pr + 1) * 256] = qd[2 * pr][None, :]
        qdm[64:128, pr * 256:(pr + 1) * 256] = qd[2 * pr + 1][None, :]
    kdm = np.zeros((128, 2 * 2048), np.float32)
    for jh in range(2):
        for pr in range(16):
            for hh in range(2):
                col = jh * 2048 + pr * 128 + hh * 64
                kdm[:, col:col + 64] = kd[2 * pr + hh, jh * 128:(jh + 1) * 128][:, None]
    bdm = np.zeros((128, 1024), np.float32)
    for hg in range(2):
        for hc in range(16):
            bdm[hg * 64:(hg + 1) * 64, hc * 64:(hc + 1) * 64] = bd[2 * hc + hg]
    tfull = np.arange(512, dtype=np.float64)
    qdf = np.exp(-slope[:, None] * (tfull[None, :] + 1.0))
    qd512 = np.zeros((128, 16 * 512), np.float32)
    for pr in range(16):
        qd512[0:64, pr * 512:pr * 512 + 512] = mask_c * qdf[2 * pr][None, :]
        qd512[64:128, pr * 512:pr * 512 + 512] = mask_c * qdf[2 * pr + 1][None, :]
    return dict(ddT=ddT.astype(BF), qdm=qdm.astype(BF), kdm=kdm.astype(BF),
                bdm=bdm, qd512=qd512.astype(BF))


ENGS = ("sync", "scalar", "vector", "tensor", "gpsimd")


class Plan:
    def __init__(self):
        self.acts = {e: [] for e in ENGS}
        self.counts = {}

    def do(self, eng, fn, waits=(), incs=()):
        ws = tuple((s, v) for w in waits if w is not None for (s, v) in (w,) if v > 0)
        self.acts[eng].append((fn, ws, tuple(incs)))
        for s, d in incs:
            self.counts[s] = self.counts.get(s, 0) + d

    def tok(self, sem):
        return (sem, self.counts.get(sem, 0))

    def emit(self, eng_name, eng, sems):
        hwm = {}
        for fn, waits, incs in self.acts[eng_name]:
            for s, v in waits:
                if hwm.get(s, 0) < v:
                    eng.wait_ge(sems[s], v)
                    hwm[s] = v
            inst = fn(eng)
            for s, d in incs:
                inst = inst.then_inc(sems[s], d)


class Arena:
    def __init__(self, tile, total_cols):
        self.t = tile
        self.total = total_cols
        self.off = 0
        self.regions = {}

    def alloc(self, name, cols):
        cols = (cols + 1) & ~1
        assert self.off + cols <= self.total, f"arena overflow at {name}"
        self.regions[name] = (self.off, cols)
        self.off += cols
        return self.regions[name]

    def bf(self, name, cols=None, off=0):
        o, c = self.regions[name]
        if cols is None:
            cols = c - off
        assert off + cols <= c, f"bf {name} oob"
        return self.t[:, o + off:o + off + cols]

    def bfr(self, name, rows, cols, off=0):
        o, c = self.regions[name]
        assert off + cols <= c
        return self.t[0:rows, o + off:o + off + cols]

    def f32(self, name, cols=None, off=0):
        o, c = self.regions[name]
        if cols is None:
            cols = c // 2 - off
        assert 2 * (off + cols) <= c, f"f32 {name} oob"
        return self.t[:, o + 2 * off:o + 2 * (off + cols)].bitcast(F32)

    def i32(self, name, cols=None, off=0):
        o, c = self.regions[name]
        if cols is None:
            cols = c // 2 - off
        assert 2 * (off + cols) <= c, f"i32 {name} oob"
        return self.t[:, o + 2 * off:o + 2 * (off + cols)].bitcast(I32)


def build_nc(debug=(), stop_after=None, dq_on=True):
    nc = bass.Bass("TRN2", target_bir_lowering=False)

    x_fm = nc.dram_tensor("x_fm", [128, NCH, T], F32, kind="ExternalInput")
    qkvT = nc.dram_tensor("qkvT", [DM, 3 * DM], F32, kind="ExternalInput")
    gateT = nc.dram_tensor("gateT", [DM, DM], F32, kind="ExternalInput")
    outT = nc.dram_tensor("outT", [DM, DM], F32, kind="ExternalInput")
    normw = nc.dram_tensor("normw", [128, NCH], F32, kind="ExternalInput")
    gup = nc.dram_tensor("gup", [2 * ID_, GU_W], I32, kind="ExternalInput")
    gusc = nc.dram_tensor("gusc", [2 * ID_, 16], F16, kind="ExternalInput")
    dnp = nc.dram_tensor("dnp", [DM, DN_W], I32, kind="ExternalInput")
    dnsc = nc.dram_tensor("dnsc", [DM, 44], F16, kind="ExternalInput")
    ddT_d = nc.dram_tensor("ddT", [128, H * 2 * 256], BF16, kind="ExternalInput")
    qdm_d = nc.dram_tensor("qdm", [128, 16 * 256], BF16, kind="ExternalInput")
    kdm_d = nc.dram_tensor("kdm", [128, 2 * 2048], BF16, kind="ExternalInput")
    bdm_d = nc.dram_tensor("bdm", [128, 1024], F32, kind="ExternalInput")
    qd512_d = nc.dram_tensor("qd512", [128, 16 * 512], BF16, kind="ExternalInput")
    onec_d = nc.dram_tensor("onec", [128, 8], BF16, kind="ExternalInput")
    oner_d = nc.dram_tensor("oner", [1, 128], BF16, kind="ExternalInput")
    ident_d = nc.dram_tensor("ident", [128, 128], BF16, kind="ExternalInput")

    yo = nc.dram_tensor("yo", [128, NCH, T], F32, kind="ExternalOutput")

    q_st = nc.dram_tensor("q_st", [128, NCH, T], BF16, kind="Internal")
    y_st = nc.dram_tensor("y_st", [128, NCH, T], BF16, kind="Internal")
    wgu = nc.dram_tensor("wgu", [22, 16, 128, 512], BF16, kind="Internal")
    wdn = nc.dram_tensor("wdn", [DN_MC, 128, 44, 128], BF16, kind="Internal")
    fin = nc.dram_tensor("fin", [128, 1024], F32, kind="Internal")
    fgo = nc.dram_tensor("fgo", [256, 1024], F32, kind="Internal")

    dbg = {}
    dbg_specs = {"y": ([128, NCH, T], BF16), "q": ([128, NCH, T], BF16),
                 "F": ([128, 1024], F32), "wgu": ([22, 16, 128, 512], BF16),
                 "wdn": ([DN_MC, 128, 44, 128], BF16),
                 "kp": ([128, 4096], BF16), "vp": ([128, 4096], BF16),
                 "qp": ([128, 512], BF16), "S1": ([128, 1024], F32)}
    for name in debug:
        sh, dt = dbg_specs[name]
        dbg[name] = nc.dram_tensor("dbg_" + name, sh, dt, kind="ExternalOutput")

    es = ExitStack()
    AR_COLS = 100352
    arena = es.enter_context(nc.sbuf_tensor("arena", [128, AR_COLS], BF16))
    ar = Arena(arena, AR_COLS)

    ar.alloc("scgu", GU_MC * 16 * 2)
    ar.alloc("scdn", DN_MC * 44 * 2)
    ar.alloc("oner", 128)
    ar.alloc("ident", 128)
    ar.alloc("dq_pk", 2 * 256 * 2)
    ar.alloc("dq_t", 6 * 256 * 2)
    ar.alloc("dq_v", 2048 * 2)
    ar.alloc("dq_vb", 2048)
    ar.alloc("dq_tr", 1024)
    ar.regions["scst"] = (ar.regions["dq_v"][0], GU_MC * 16)
    p0_base = ar.off
    ar.alloc("ddr", 4 * 1024)
    ar.alloc("qdm", 16 * 256)
    ar.alloc("kdm", 2 * 2048)
    ar.alloc("bdm", 1024 * 2)
    ar.alloc("S", 1024 * 2)
    ar.alloc("Sbf", 1024)
    ar.alloc("qkv", 48 * TC)
    ar.alloc("wbf", 8 * 512)
    ar.alloc("wf32", 4 * 512 * 2)
    ar.alloc("xr", 2 * 512 * 2)
    ar.alloc("xb", NCH * TC)
    ar.alloc("kp", 2 * 2048)
    ar.alloc("vp", 2 * 2048)
    ar.alloc("qp", 2 * 256)
    ar.alloc("awb", 2 * 1024)
    ar.alloc("yr", 4 * 256)
    p0_end = ar.off
    ar.off = p0_base
    ar.alloc("x32", NCH * TT * 2)
    ar.alloc("xb1", NCH * TT)
    ar.alloc("yf", NCH * TT)
    ar.alloc("qr", 4 * TT)
    ar.alloc("zr", 2 * TT)
    ar.alloc("sqr", 2 * TT)
    ar.alloc("Fbf", 1024)
    ar.alloc("Fsc", 1024)
    ar.alloc("wat", 8 * 512 + 2 * 512 * 2)
    ar.alloc("gp", NCH * TT)
    ar.alloc("g1", 2 * TT * 2)
    ar.alloc("cmb", 2 * TT * 2)
    ar.alloc("ag", NCH * TT)
    ar.alloc("rows", 4096)
    ar.alloc("hs", 2 * 1024)
    ar.alloc("ybr", 2 * TT)
    p1_end = ar.off
    ar.off = max(p0_end, p1_end)
    ar.alloc("y32x", 3 * 2048)
    ar.alloc("dnw", 2 * 2816)
    ar.alloc("normw", NCH * 2)
    ar.alloc("ones", 8)
    assert ar.off <= AR_COLS, f"arena {ar.off} > {AR_COLS}"
    ar.regions["gw"] = (ar.regions["wat"][0], 2 * 2048)
    ar.regions["mrb"] = (ar.regions["dnw"][0], 2048)
    ar.regions["qd512"] = ar.regions["ag"]

    def y32slot(mc):
        if mc < 4:
            return ar.f32("xb1", 512, mc * 512)
        if mc < 8:
            return ar.f32("ag", 512, (mc - 4) * 512)
        if mc < 13:
            return ar.f32("wat", 512, (mc - 8) * 512)
        return ar.f32("y32x", 512, (mc - 13) * 512)

    H_REGS = [("gp", 16), ("yf", 16), ("qr", 4), ("zr", 2), ("sqr", 2), ("g1", 4)]

    def hslot(c):
        for rname, n in H_REGS:
            if c < n:
                return ar.bf(rname, 512, c * 512)
            c -= n
        raise AssertionError

    def rows_f32(slot, n=512):
        o = ar.regions["rows"][0]
        return arena[0:1, o + slot * 1024:o + slot * 1024 + 2 * n].bitcast(F32)

    def rhs_bf(off, n=512):
        o = ar.regions["rows"][0]
        return arena[0:1, o + 3072 + off:o + 3072 + off + n]

    PS = [es.enter_context(nc.psum_tensor(f"ps{i}", [128, 512], F32)) for i in range(6)]
    PSW = es.enter_context(nc.psum_tensor("psw", [128, 1024], F32))
    PSWb = PSW[:].bitcast(BF16)
    PS5b = PS[5][:].bitcast(BF16)

    sem_names = (["dS", "dA", "sV", "sT", "sA", "sP", "sC", "dC", "dK", "dV",
                  "dQS", "dOUT", "dYL", "dPK0", "dPK1", "dTR", "dDQ", "dQ5"]
                 + [f"dX{i}" for i in range(2)] + [f"dW{i}" for i in range(4)]
                 + [f"dDD{i}" for i in range(4)] + [f"dY{i}" for i in range(4)]
                 + [f"dQ{i}" for i in range(4)] + [f"dGW{i}" for i in range(2)]
                 + [f"dDN{i}" for i in range(2)])
    sems = {n: es.enter_context(nc.semaphore(n)) for n in sem_names}
    p = Plan()
    tok = p.tok

    def dma(queue, dst, src, waits=(), sem=None, transpose=False):
        qsem = sem or {"sync": "dS", "scalar": "dA"}[queue]
        p.do(queue, lambda e, d=dst, s=src, t=transpose: e.dma_start(d, s, transpose=t),
             waits, [(qsem, 16)])
        return tok(qsem)

    def cV(fn, waits=()):
        p.do("vector", fn, waits, [("sV", 1)])
        return tok("sV")

    def cT(fn, waits=()):
        p.do("tensor", fn, waits, [("sT", 1)])
        return tok("sT")

    def cA(fn, waits=()):
        p.do("scalar", fn, waits, [("sA", 1)])
        return tok("sA")

    def cP(fn, waits=()):
        p.do("gpsimd", fn, waits, [("sP", 1)])
        return tok("sP")

    onec = arena[:, ar.regions["ones"][0]:ar.regions["ones"][0] + 1]
    oner = arena[0:1, ar.regions["oner"][0]:ar.regions["oner"][0] + 128]
    ident = arena[:, ar.regions["ident"][0]:ar.regions["ident"][0] + 128]

    def Sbf_ap(h):
        o = ar.regions["Sbf"][0]
        r = (h % 2) * 64
        c = (h // 2) * 64
        return arena[r:r + 64, o + c:o + c + 64]

    def Fsc_ap(h):
        o = ar.regions["Fsc"][0]
        r = (h % 2) * 64
        c = (h // 2) * 64
        return arena[r:r + 64, o + c:o + c + 64]

    def PSW_S(h):
        r = (h % 2) * 64
        c = (h // 2) * 64
        return PSW[r:r + 64, c:c + 64]

    dma("sync", ar.bf("qdm"), qdm_d.ap(), sem="dC")
    dma("sync", ar.bf("kdm"), kdm_d.ap(), sem="dC")
    dma("sync", ar.f32("bdm"), bdm_d.ap(), sem="dC")
    dma("sync", ar.f32("normw"), normw.ap(), sem="dC")
    dma("sync", arena[:, ar.regions["ones"][0]:ar.regions["ones"][0] + 8], onec_d.ap(), sem="dC")
    dma("sync", arena[0:1, ar.regions["oner"][0]:ar.regions["oner"][0] + 128], oner_d.ap(), sem="dC")
    dma("sync", arena[:, ar.regions["ident"][0]:ar.regions["ident"][0] + 128], ident_d.ap(), sem="dC")
    scst16 = arena[:, ar.regions["scst"][0]:ar.regions["scst"][0] + GU_MC * 16].bitcast(F16)
    t_ld = dma("sync", scst16.rearrange("p (c s) -> p c s", s=16),
               gusc.ap().rearrange("(c p) s -> p c s", p=128), sem="dC")
    t_scgu = cV(lambda e: e.tensor_copy(ar.f32("scgu"), scst16), [t_ld])
    scst16d = arena[:, ar.regions["scst"][0]:ar.regions["scst"][0] + DN_MC * 44].bitcast(F16)
    t_ld2 = dma("sync", scst16d.rearrange("p (c s) -> p c s", s=44),
                dnsc.ap().rearrange("(c p) s -> p c s", p=128), waits=[t_scgu], sem="dC")
    t_scdn = cV(lambda e: e.tensor_copy(ar.f32("scdn"), scst16d), [t_ld2])
    t_S0 = cV(lambda e: e.memset(ar.f32("S"), 0.0))
    t_Sbf = cP(lambda e: e.memset(ar.bf("Sbf"), 0.0))
    t_consts = tok("dC")

    gu_done = [None] * GU_MC
    dn_done = [None] * (DN_MC * 4)
    dq = {"pk_par": 0, "pk_war": [None, None], "v_rd": None,
          "vb_rd": None, "tr_dma": None}

    def _dq_nib_dve(pk, W, extra_waits):
        vi = ar.i32("dq_v", 8 * W)
        ts_ = [ar.i32("dq_t", W, i * 256) for i in range(6)]
        w0 = list(extra_waits)
        for j in range(8):
            def f(e, j=j, pk=pk, W=W, vi=vi, ts_=ts_):
                t, u, sb, m1, pp, g = ts_
                e.tensor_scalar(t[:], pk, 4 * j, None, op0=ALU.logical_shift_right)
                e.tensor_scalar(u[:], t[:], 7, None, op0=ALU.bitwise_and)
                e.tensor_scalar(sb[:], t[:], 8, 28, op0=ALU.bitwise_and, op1=ALU.logical_shift_left)
                e.tensor_scalar(m1[:], u[:], 1, None, op0=ALU.min)
                e.scalar_tensor_tensor(pp[:], u[:], 251, m1[:], op0=ALU.add, op1=ALU.mult)
                e.tensor_scalar(g[:], u[:], 2, None, op0=ALU.is_ge)
                e.tensor_tensor(pp[:], pp[:], g[:], op=ALU.add)
                e.tensor_scalar(pp[:], pp[:], 22, None, op0=ALU.logical_shift_left)
                return e.tensor_tensor(vi.rearrange("p (w k) -> p w k", k=8)[:, :, j],
                                       pp[:], sb[:], op=ALU.bitwise_or)
            last = cV(f, w0)
            w0 = []
        return last, "sV"

    def _dq_nib_pool(pk, W, extra_waits):
        vf = ar.i32("dq_v", 8 * W).bitcast(F32)
        n_i = ar.i32("dq_t", W, 0)
        nf = ar.f32("dq_t", W, 256)
        s = ar.f32("dq_t", W, 512)
        u = ar.f32("dq_t", W, 768)
        r4 = ar.f32("dq_t", W, 1024)
        r6 = ar.f32("dq_t", W, 1280)
        w0 = list(extra_waits)
        last = None
        for j in range(8):
            t_n = cV(lambda e, j=j: e.tensor_scalar(
                n_i[:], pk, 4 * j, 15, op0=ALU.logical_shift_right, op1=ALU.bitwise_and),
                w0 + ([last] if last else []))
            w0 = []
            def fp(e, j=j):
                e.tensor_copy(nf[:], n_i[:])
                e.tensor_scalar(s[:], nf[:], 8.0, None, op0=ALU.is_ge)
                e.tensor_scalar(u[:], s[:], -8.0, None, op0=ALU.mult)
                e.tensor_tensor(u[:], u[:], nf[:], op=ALU.add)
                e.tensor_scalar(r4[:], u[:], -4.0, None, op0=ALU.add)
                e.tensor_scalar(r6[:], r4[:], -2.0, None, op0=ALU.add)
                e.tensor_scalar(r4[:], r4[:], 0.0, None, op0=ALU.max)
                e.tensor_scalar(r6[:], r6[:], 0.0, None, op0=ALU.max)
                e.tensor_tensor(r4[:], r4[:], u[:], op=ALU.add)
                e.tensor_tensor(r6[:], r6[:], r6[:], op=ALU.add)
                e.tensor_tensor(r4[:], r4[:], r6[:], op=ALU.add)
                e.tensor_scalar(r4[:], r4[:], 0.5, None, op0=ALU.mult)
                e.tensor_scalar(s[:], s[:], -2.0, None, op0=ALU.mult)
                e.tensor_scalar(s[:], s[:], 1.0, None, op0=ALU.add)
                return e.tensor_tensor(vf.rearrange("p (w k) -> p w k", k=8)[:, :, j],
                                       r4[:], s[:], op=ALU.mult)
            last = cP(fp, [t_n])
        return last, "sP"

    def _dq_chunk(pk_src, W, scview, goff, ngroups, grsz, dst, use_pool):
        par = dq["pk_par"]
        dq["pk_par"] ^= 1
        pk = ar.i32("dq_pk", W, par * 256)
        wt = [w for w in (dq["pk_war"][par],) if w]
        tpk = dma("scalar", pk, pk_src, waits=wt, sem=f"dPK{par}")
        extra = [tpk]
        if dq["v_rd"]:
            extra.append(dq["v_rd"])
        if use_pool:
            t_nib, _ = _dq_nib_pool(pk, W, extra)
        else:
            t_nib, _ = _dq_nib_dve(pk, W, extra)
        dq["pk_war"][par] = t_nib
        wv = [t_nib] + ([dq["vb_rd"]] if dq["vb_rd"] else [])
        for g in range(ngroups):
            def fs(e, g=g):
                return e.tensor_scalar(
                    ar.bf("dq_vb", grsz, g * grsz),
                    ar.i32("dq_v", 8 * W).bitcast(F32)[:, g * grsz:(g + 1) * grsz],
                    scview[:, goff + g:goff + g + 1], None, op0=ALU.mult)
            t_sc = cV(fs, wv)
            wv = []
        dq["v_rd"] = t_sc
        nkc = ngroups
        t_out = None
        for r in range((nkc + 7) // 8):
            kcs = list(range(r * 8, min(nkc, r * 8 + 8)))
            for kc in kcs:
                wtt = []
                if kc == kcs[0]:
                    wtt = [t_sc] + ([dq["tr_dma"]] if dq["tr_dma"] else [])
                def ftr(e, kc=kc):
                    return e.dma_start(ar.bf("dq_tr", 128, (kc % 8) * 128),
                                       ar.bf("dq_vb", 128, kc * 128), transpose=True)
                p.do("scalar", ftr, wtt, [("dTR", 16)])
            t_tr = tok("dTR")
            n = len(kcs)
            t_out = dma("scalar", dst[:, r * 8:r * 8 + n, :],
                        ar.bf("dq_tr", n * 128).rearrange("p (k m) -> p k m", m=128),
                        waits=[t_tr], sem="dDQ")
            dq["tr_dma"] = t_out
        dq["vb_rd"] = tok("dTR")
        return t_out

    def plan_dequant_gu(mc, use_pool):
        dst = wgu.ap()[mc // 4, :, :, (mc % 4) * 128:(mc % 4) * 128 + 128]
        dst = dst.rearrange("k p m -> p k m")
        t = _dq_chunk(gup.ap()[mc * 128:(mc + 1) * 128, :], GU_W,
                      ar.f32("scgu", 16, mc * 16), 0, 16, 128, dst, use_pool)
        gu_done[mc] = t

    def plan_dequant_dn(qq, use_pool):
        M, quar = qq // 4, qq % 4
        dst = wdn.ap()[M, :, quar * 11:quar * 11 + 11, :]
        t = _dq_chunk(dnp.ap()[M * 128:(M + 1) * 128, quar * 176:(quar + 1) * 176], 176,
                      ar.f32("scdn", 44, M * 44), quar * 11, 11, 128, dst, use_pool)
        dn_done[qq] = t

    dq_queue = []
    gi = di = 0
    while gi < GU_MC or di < DN_MC * 4:
        if di * 88 >= gi * 64 and gi < GU_MC or di >= DN_MC * 4:
            dq_queue.append(("gu", gi)); gi += 1
        else:
            dq_queue.append(("dn", di)); di += 1
    dq_pos = [0]
    POOL_SHARE = 0.0

    def pump_dequant(n):
        for _ in range(n):
            if dq_pos[0] >= len(dq_queue):
                return
            kind, idx = dq_queue[dq_pos[0]]
            use_pool = (dq_pos[0] % 20) < int(20 * POOL_SHARE)
            (plan_dequant_gu if kind == "gu" else plan_dequant_dn)(idx, use_pool)
            dq_pos[0] += 1

    qkvv = ar.bf("qkv")
    xbv = ar.bf("xb")
    kpv = ar.bf("kp")
    vpv = ar.bf("vp")

    t_xr = [None, None]
    t_wf = [None] * 4
    t_wbf = [None] * 16
    t_wbf_rd = [None] * 16
    t_xb = [None] * 16
    t_ps_evac = [None] * 4
    t_qkv_rd_prev = None
    t_kp = None
    t_vp = None
    t_Supd = t_S0
    t_awdd = [None, None]
    t_inter = [None, None]
    t_yev = [None] * 4
    t_ydma = [None] * 4
    t_qp = [None, None]
    t_ds_last = None
    t_scan_aw_rd = [None, None, None, None]
    t_ddm_slot = [None] * 4
    t_dddma = [None] * 4

    def dd_issue(pr):
        slot = pr % 4
        t_dddma[slot] = dma("sync", ar.bf("ddr", 1024, slot * 1024),
                            ddT_d.ap()[:, pr * 1024:(pr + 1) * 1024],
                            waits=[w for w in (t_ddm_slot[slot],) if w], sem=f"dDD{slot}")

    for pr0 in range(4):
        dd_issue(pr0)

    NCC = 1 if stop_after in ("p0c1", "qkv", "tr", "aw", "qst", "trnq", "blk1") else NC0
    for cc in range(NCC):
        t0 = cc * TC
        for kc in range(NCH):
            xsl = ar.f32("xr", 512, (kc % 2) * 512)
            td = dma("sync", xsl, x_fm.ap()[:, kc, t0:t0 + TC],
                     waits=[w for w in (t_xr[kc % 2],) if w], sem=f"dX{kc % 2}")
            wts = [td] + ([t_qkv_rd_prev] if (kc < 2 and t_qkv_rd_prev) else [])
            t_xb[kc] = cP(lambda e, kc=kc, xs=xsl: e.tensor_copy(
                xbv[:, kc * TC:(kc + 1) * TC], xs), wts)
            t_xr[kc % 2] = t_xb[kc]
        for ms in range(12):
            for kh in range(2):
                for kc8 in range(8):
                    kc = kh * 8 + kc8
                    wsl = ar.f32("wf32", 512, (kc8 % 4) * 512)
                    td = dma("sync", wsl, qkvT.ap()[kc * 128:(kc + 1) * 128, ms * 512:(ms + 1) * 512],
                             waits=[w for w in (t_wf[kc8 % 4],) if w], sem=f"dW{kc8 % 4}")
                    t_wbf[kc8] = cA(lambda e, kc8=kc8, w=wsl: e.copy(
                        ar.bf("wbf", 512, kc8 * 512), w),
                        [td] + ([t_wbf_rd[kc8]] if t_wbf_rd[kc8] else []))
                    t_wf[kc8 % 4] = t_wbf[kc8]
                for kc8 in range(8):
                    kc = kh * 8 + kc8
                    for msub in range(4):
                        wts = [t_wbf[kc8], t_xb[kc]]
                        if kc == 0:
                            if t_ps_evac[msub]:
                                wts.append(t_ps_evac[msub])
                            if cc > 0 and ms == 0:
                                wts += [w for w in t_yev + t_awdd if w]
                        def fmm(e, kc=kc, kc8=kc8, msub=msub):
                            return e.matmul(PS[msub][:],
                                            ar.bf("wbf", 128, kc8 * 512 + msub * 128),
                                            xbv[:, kc * TC:(kc + 1) * TC],
                                            start=(kc == 0), stop=(kc == 15))
                        t_mm = cT(fmm, [w for w in wts if w])
                    t_wbf_rd[kc8] = t_mm
            for msub in range(4):
                mc = ms * 4 + msub
                wse = [t_mm]
                if ms == 0 and msub == 0 and cc > 0:
                    wse.append(tok("dQS"))
                t_ps_evac[msub] = cA(lambda e, msub=msub, mc=mc: e.activation(
                    qkvv[:, mc * TC:(mc + 1) * TC], PS[msub][:], AF.Silu), wse)
        t_qkv_rd_prev = t_mm
        t_qkv_done = tok("sA")
        if stop_after == "qkv":
            break
        if stop_after != "trnq":
            for mc in range(NCH):
                dma("scalar", q_st.ap()[:, mc, t0:t0 + TC], qkvv[:, mc * TC:(mc + 1) * TC],
                    waits=[t_qkv_done], sem="dQS")
        if stop_after == "qst":
            break
        for bi in range(2):
            wts0 = [t_qkv_done, t_Supd]
            if t_ds_last:
                wts0.append(t_ds_last)
            if t_kp:
                wts0.append(t_kp)
            for jh in range(2):
                for pr in range(16):
                    c = (16 + pr) * TC + bi * 256 + jh * 128
                    p.do("sync", lambda e, c=c, jh=jh, pr=pr: e.dma_start(
                        kpv[:, jh * 2048 + pr * 128:jh * 2048 + (pr + 1) * 128],
                        qkvv[:, c:c + 128], transpose=True),
                        wts0 if (jh == 0 and pr == 0) else (), [("dK", 16)])
            t_ktr = tok("dK")
            for jh in range(2):
                t_kp = cV(lambda e, jh=jh: e.tensor_tensor(
                    kpv[:, jh * 2048:(jh + 1) * 2048], kpv[:, jh * 2048:(jh + 1) * 2048],
                    ar.bf("kdm", 2048, jh * 2048), op=ALU.mult), [t_ktr] if jh == 0 else ())
            for jh in range(2):
                for pr in range(16):
                    c = (32 + pr) * TC + bi * 256 + jh * 128
                    p.do("sync", lambda e, c=c, jh=jh, pr=pr: e.dma_start(
                        vpv[:, jh * 2048 + pr * 128:jh * 2048 + (pr + 1) * 128],
                        qkvv[:, c:c + 128], transpose=True), (), [("dV", 16)])
            t_vp = tok("dV")
            def plan_awT(pr):
                ring = pr % 2
                for hh in range(2):
                    psp = PS[1 + ring * 2 + hh]
                    for jh in range(2):
                        wts = []
                        if hh == 0 and jh == 0:
                            wts = [w for w in (t_vp, t_scan_aw_rd[ring * 2],
                                               t_scan_aw_rd[ring * 2 + 1]) if w]
                        def fa(e, pr=pr, hh=hh, jh=jh, psp=psp, bi=bi):
                            c = (16 + pr) * TC + bi * 256 + jh * 128
                            q0 = pr * TC + bi * 256
                            return e.matmul(psp[:, jh * 256:jh * 256 + 256],
                                            qkvv[hh * 64:hh * 64 + 64, c:c + 128],
                                            qkvv[hh * 64:hh * 64 + 64, q0:q0 + 256],
                                            start=True, stop=True)
                        t_aw = cT(fa, wts)
                t_qp[pr % 2] = cP(lambda e, pr=pr, bi=bi: e.tensor_tensor(
                    ar.bf("qp", 256, (pr % 2) * 256),
                    qkvv[:, pr * TC + bi * 256:pr * TC + bi * 256 + 256],
                    ar.bf("qdm", 256, pr * 256), op=ALU.mult),
                    [t_qkv_done] + ([t_inter[pr % 2]] if t_inter[pr % 2] else []))
                return t_aw

            def plan_dd(pr, t_aw):
                ring = pr % 2
                slot = pr % 4
                for hh in range(2):
                    psp = PS[1 + ring * 2 + hh]
                    wts = [t_aw]
                    if hh == 0:
                        wts.append(t_dddma[slot])
                        if t_inter[ring]:
                            wts.append(t_inter[ring])
                    t_dd = cV(lambda e, pr=pr, hh=hh, psp=psp, ring=ring, slot=slot:
                              e.tensor_tensor(
                                  ar.bf("awb", 512, ring * 1024 + hh * 512), psp[:],
                                  ar.bf("ddr", 512, slot * 1024 + hh * 512), op=ALU.mult), wts)
                    t_scan_aw_rd[ring * 2 + hh] = t_dd
                t_awdd[ring] = t_dd
                t_ddm_slot[slot] = t_dd
                dd_issue((pr + 4) % 16)
                return t_dd

            def plan_y_dS(pr, t_dd):
                ring = pr % 2
                ytensor = PS[0] if ring == 0 else PS[5]
                first_wts = [t_dd, t_kp, t_vp, t_qp[ring], t_Sbf]
                if t_yev[(pr - 2) % 4]:
                    first_wts.append(t_yev[(pr - 2) % 4])
                first = True
                for hh in range(2):
                    h = 2 * pr + hh
                    for jh in range(2):
                        def fi(e, hh=hh, jh=jh, h=h, ring=ring, yt=ytensor):
                            return e.matmul(yt[hh * 64:hh * 64 + 64, 0:256],
                                            vpv[:, jh * 2048 + h * 64:jh * 2048 + h * 64 + 64],
                                            ar.bf("awb", 256, ring * 1024 + hh * 512 + jh * 256),
                                            start=(jh == 0), stop=False)
                        cT(fi, [w for w in first_wts if w] if first else [])
                        first = False
                    def fint(e, hh=hh, h=h, yt=ytensor):
                        return e.matmul(yt[hh * 64:hh * 64 + 64, 0:256],
                                        Sbf_ap(h),
                                        ar.bf("qp", 256, (0 if yt is PS[0] else 1) * 256)[hh * 64:hh * 64 + 64, :],
                                        start=False, stop=True)
                    t_y = cT(fint)
                t_inter[ring] = t_y
                ysl = ar.bf("yr", 256, (pr % 4) * 256)
                t_ev = cA(lambda e, ysl=ysl, yt=ytensor: e.copy(ysl, yt[:, 0:256]),
                    [t_y] + ([t_ydma[pr % 4]] if t_ydma[pr % 4] else []))
                t_yev[pr % 4] = t_ev
                t_ydma[pr % 4] = dma("scalar",
                                     y_st.ap()[:, pr, t0 + bi * 256:t0 + bi * 256 + 256],
                                     ysl, waits=[t_ev], sem=f"dY{pr % 4}")
                for hh in range(2):
                    h = 2 * pr + hh
                    for jh in range(2):
                        def fds(e, h=h, jh=jh):
                            return e.matmul(PSW_S(h),
                                            kpv[:, jh * 2048 + h * 64:jh * 2048 + h * 64 + 64],
                                            vpv[:, jh * 2048 + h * 64:jh * 2048 + h * 64 + 64],
                                            start=(jh == 0), stop=(jh == 1))
                        t_ds = cT(fds, [t_y] if (hh == 0 and jh == 0) else [])
                return t_ds

            if stop_after == "aw":
                for pr in range(16):
                    plan_dd(pr, plan_awT(pr))
                continue
            t_aw_p = plan_awT(0)
            t_dd_p = plan_dd(0, t_aw_p)
            for pr in range(1, 16):
                t_aw_n = plan_awT(pr)
                t_ds_last = plan_y_dS(pr - 1, t_dd_p)
                t_dd_p = plan_dd(pr, t_aw_n)
            t_ds_last = plan_y_dS(15, t_dd_p)
            cV(lambda e: e.tensor_tensor(ar.f32("S"), ar.f32("S"), ar.f32("bdm"), op=ALU.mult),
               [t_ds_last])
            t_Supd = cV(lambda e: e.tensor_tensor(ar.f32("S"), ar.f32("S"), PSW[:], op=ALU.add))
            t_Sbf = cP(lambda e: e.tensor_copy(ar.bf("Sbf"), ar.f32("S")), [t_Supd])
            if cc == 0 and bi == 0 and dbg:
                if "kp" in dbg:
                    dma("scalar", dbg["kp"].ap(), kpv[:], waits=[t_kp])
                if "vp" in dbg:
                    dma("scalar", dbg["vp"].ap(), vpv[:], waits=[t_vp])
                if "qp" in dbg:
                    dma("scalar", dbg["qp"].ap(), ar.bf("qp"), waits=[t_qp[0], t_qp[1]])
                if "S1" in dbg:
                    dma("scalar", dbg["S1"].ap(), ar.f32("S"), waits=[t_Supd])
            if stop_after == "blk1":
                break
        if dq_on:
            pump_dequant(19)
    if dq_on:
        pump_dequant(10 ** 6)

    if stop_after in ("p0c1", "qkv", "tr", "aw", "qst", "trnq", "blk1"):
        dbg_w0 = [tok(s) for s in sem_names if s.startswith("d")]
        if "y" in dbg:
            dma("scalar", dbg["y"].ap(), y_st.ap(), waits=dbg_w0)
        if "q" in dbg:
            dma("scalar", dbg["q"].ap(), q_st.ap(), waits=dbg_w0)
        final_waits = [tok(s) for s in sem_names if s.startswith("d")]
        p.do("vector", lambda e: e.engine_nop(), final_waits, ())
        with nc.Block() as block:
            block.sync(lambda e: p.emit("sync", e, sems))
            block.scalar(lambda e: p.emit("scalar", e, sems))
            block.vector(lambda e: p.emit("vector", e, sems))
            block.tensor(lambda e: p.emit("tensor", e, sems))
            block.gpsimd(lambda e: p.emit("gpsimd", e, sems))
        es.close()
        return nc
    t_fin = dma("scalar", fin.ap(), ar.f32("S"), waits=[t_Supd], sem="dC")
    p.do("gpsimd", lambda e: e.collective_compute(
        "AllGather", ALU.bypass, replica_groups=[[0, 1], [2, 3], [4, 5], [6, 7]],
        ins=[fin.ap().opt()], outs=[fgo.ap().opt()]), [t_fin], [("sC", 1)])
    t_coll = tok("sC")
    if "F" in dbg:
        dma("scalar", dbg["F"].ap(), ar.f32("S"), waits=[t_Supd])

    t_F = dma("sync", ar.f32("cmb", 1024), fgo.ap()[0:128, :], waits=[t_coll], sem="dC")
    t_Fbf = cP(lambda e: e.tensor_copy(ar.bf("Fbf"), ar.f32("cmb", 1024)), [t_F])

    slope, _, _, _, _ = _decay_tables()
    Avals = np.exp(-slope[:, None] * (512.0 * np.arange(NT1)[None, :])).astype(np.float32)

    t_qr = [None] * 4
    t_zr = [None, None]
    t_sqw = [None, None]
    t_yfw = [None] * 16
    t_corr = [None, None]
    t_gw = [None, None]
    t_dnw = [None, None]
    t_hs = [None, None]
    t_ybw = [None, None]
    t_watc = [None] * 16
    t_watr = [None] * 16
    t_wfat = [None, None]
    t_psw_user = None
    t_prev_out = None
    t_prev_x32_rd = None

    def newton(stat_ap, scale, eps, waits):
        v = rows_f32(0)
        y = rows_f32(1)
        t1 = rows_f32(2)
        cV(lambda e: e.tensor_scalar(v, stat_ap, scale, eps, op0=ALU.mult, op1=ALU.add), waits)
        cV(lambda e: e.tensor_scalar(y.bitcast(I32), v.bitcast(I32), 1, None,
                                     op0=ALU.logical_shift_right))
        cV(lambda e: e.tensor_scalar(y.bitcast(I32), y.bitcast(I32), -1, 0x5F3759DF,
                                     op0=ALU.mult, op1=ALU.add))
        tk = None
        for _ in range(3):
            cV(lambda e: e.tensor_tensor(t1, y, y, op=ALU.mult))
            cV(lambda e: e.tensor_tensor(t1, t1, v, op=ALU.mult))
            cV(lambda e: e.tensor_scalar(t1, t1, -0.5, 1.5, op0=ALU.mult, op1=ALU.add))
            tk = cV(lambda e: e.tensor_tensor(y, y, t1, op=ALU.mult))
        return tk, y

    def bcast(vals, waits, slot0=0):
        nonlocal t_psw_user
        n = len(vals)
        assert n == 1
        t_rhs = None
        for i, v in enumerate(vals):
            hi = rhs_bf(0, 512)
            lo = rhs_bf(512, 512)
            lov = rows_f32(2)
            cV(lambda e, v=v, hi=hi: e.tensor_copy(hi, v), waits if i == 0 else [])
            cV(lambda e, v=v, hi=hi, lov=lov: e.tensor_tensor(lov, v, hi, op=ALU.subtract))
            t_rhs = cV(lambda e, lo=lo, lov=lov: e.tensor_copy(lo, lov))
        outs = []
        t_last = None
        for i in range(n):
            for half in range(2):
                src = rhs_bf(half * 512, 512)
                wts = []
                if half == 0:
                    wts = ([t_rhs] if i == 0 else []) + ([t_psw_user] if t_psw_user else [])
                t_b = cT(lambda e, src=src, half=half: e.matmul(
                    PSW[:, half * 512:half * 512 + 512], oner, src, start=True, stop=True), wts)
            t_ev = cA(lambda e: e.copy(ar.f32("mrb", 1024), PSW[:]),
                      [t_b] + ([t_last] if t_last else []))
            dst = ar.f32("cmb", 512, slot0 * 512)
            t_last = cV(lambda e, dst=dst: e.tensor_tensor(
                dst, ar.f32("mrb", 512, 0), ar.f32("mrb", 512, 512), op=ALU.add), [t_ev])
            outs.append(dst)
            t_psw_user = t_ev
        return t_last, outs

    t_prev_down = None
    NTT = {"p0": 0, "p1t1": 1}.get(stop_after, NT1)
    for tl in range(NTT):
        t0 = tl * TT
        guard = [w for w in (t_prev_out, t_prev_down) if w]
        wx = list(guard) + ([t_prev_x32_rd] if t_prev_x32_rd else [])
        if tl == 0:
            wx = [t_fin, tok("sV"), tok("sT"), tok("sA"), tok("sP")]
        t_xd = dma("sync", ar.f32("x32").rearrange("p (c t) -> p c t", t=TT),
                   x_fm.ap()[:, :, t0:t0 + TT], waits=wx, sem="dX0")
        t_xbc = [None] * NCH
        for mc in range(NCH):
            wts = [t_xd]
            if mc == 0:
                wts += guard + ([tok("sT")] if tl > 0 else [])
            t_xbc[mc] = cP(lambda e, mc=mc: e.tensor_copy(
                ar.bf("xb1", 512, mc * 512), ar.f32("x32", 512, mc * 512)), wts)
        wq5 = list(guard)
        if tl == 0:
            wq5 = [tok("sT"), tok("sV"), tok("sA")]
        t_qd512 = dma("sync", ar.bf("qd512"), qd512_d.ap(), waits=wq5, sem="dQ5")
        t_Fsc = None
        for hg in range(2):
            for hc in range(16):
                hdx = 2 * hc + hg
                a = float(Avals[hdx, tl])
                wts = [t_Fbf] if (hg == 0 and hc == 0) else []
                if hg == 0 and hc == 0 and tl > 0:
                    wts.append(tok("sT"))
                t_Fsc = cP(lambda e, hg=hg, hc=hc, a=a: e.tensor_scalar(
                    Fsc_ap(2 * hc + hg), arena[hg * 64:hg * 64 + 64,
                                               ar.regions["Fbf"][0] + hc * 64:
                                               ar.regions["Fbf"][0] + hc * 64 + 64],
                    a, None, op0=ALU.mult), wts)
        t_statq = None
        for pr in range(16):
            wy = [w for w in (t_yfw[pr],) if w] + (guard if pr == 0 else [])
            if tl == 0 and pr == 0:
                wy += [tok("dY0"), tok("dY1"), tok("dY2"), tok("dY3")]
            dma("scalar", ar.bf("yf", 512, pr * 512), y_st.ap()[:, pr, t0:t0 + TT],
                waits=wy, sem="dYL")
        t_yall = tok("dYL")
        for pr in range(16):
            qsl = ar.bf("qr", 512, (pr % 4) * 512)
            wq = [w for w in (t_qr[pr % 4],) if w] + (guard if pr == 0 else [])
            if tl == 0 and pr == 0:
                wq.append(tok("dQS"))
            tq = dma("scalar", qsl, q_st.ap()[:, pr, t0:t0 + TT],
                     waits=wq, sem=f"dQ{pr % 4}")
            ysl = ar.bf("yf", 512, pr * 512)
            ty = t_yall
            zsl = ar.bf("zr", 512, (pr % 2) * 512)
            t_z = cP(lambda e, q=qsl, z=zsl, pr=pr: e.tensor_tensor(
                z, q, ar.bf("qd512", 512, pr * 512), op=ALU.mult),
                [tq, t_Fsc, t_qd512] + ([t_zr[pr % 2]] if t_zr[pr % 2] else [])
                + (guard if pr < 2 else []))
            t_zr[pr % 2] = t_z
            t_qr[pr % 4] = t_z
            cps = PS[3 + pr % 2]
            for hh in range(2):
                h = 2 * pr + hh
                wts = [t_z]
                if hh == 0:
                    if t_corr[pr % 2]:
                        wts.append(t_corr[pr % 2])
                    if pr < 2:
                        wts += guard
                    if tl == 0 and pr < 2:
                        wts += [w for w in t_awdd if w]
                t_cm = cT(lambda e, hh=hh, h=h, z=zsl, cps=cps: e.matmul(
                    cps[hh * 64:hh * 64 + 64, :], Fsc_ap(h),
                    z[hh * 64:hh * 64 + 64, :], start=True, stop=True), wts)
            t_yfull = cV(lambda e, y=ysl, cps=cps: e.tensor_tensor(y, y, cps[:], op=ALU.add),
                         [t_cm, ty])
            t_corr[pr % 2] = t_yfull
            t_yfw[pr] = t_yfull
            sqs = ar.bf("sqr", 512, (pr % 2) * 512)
            t_sq = cP(lambda e, y=ysl, s=sqs: e.tensor_tensor(s, y, y, op=ALU.mult),
                      [t_yfull] + ([t_sqw[pr % 2]] if t_sqw[pr % 2] else [])
                      + (guard if pr < 2 else []))
            wts = [t_sq]
            if pr == 0:
                wts.append(t_Supd)
                if t_psw_user:
                    wts.append(t_psw_user)
            t_statq = cT(lambda e, s=sqs, pr=pr: e.matmul(
                PSW[0:1, 0:512], onec, s, start=(pr == 0), stop=(pr == 15)), wts)
            t_sqw[pr % 2] = t_statq
        wts_n = [t_statq] + ([t_Fbf] if tl == 0 else [])
        t_n, rsd = newton(PSW[0:1, 0:512], 1.0 / DM, EPS_ATTN, wts_n)
        t_cmby, (rby,) = bcast([rsd], [t_n])
        gp_done = [None] * 16
        t_g1 = [None, None]
        t_bank_rd = [None] * 4
        for sl in range(4):
            for kh in range(2):
                for kc8 in range(8):
                    kc = kh * 8 + kc8
                    wsl = ar.f32("wat", 512, 2048 + (kc8 % 2) * 512)
                    wdd = [w for w in (t_wfat[kc8 % 2],) if w]
                    if sl == 0 and kh == 0 and kc8 < 2:
                        wdd += guard
                        if tl == 0:
                            wdd.append(tok("sA"))
                    td = dma("sync", wsl, gateT.ap()[kc * 128:(kc + 1) * 128, sl * 512:(sl + 1) * 512],
                             waits=wdd, sem=f"dW{kc8 % 2}")
                    wts = [td] + ([t_watr[kc8]] if t_watr[kc8] else [])
                    if sl == 0 and kh == 0 and kc8 < 2:
                        wts += guard
                    t_watc[kc8] = cA(lambda e, kc8=kc8, w=wsl: e.copy(
                        ar.bf("wat", 512, kc8 * 512), w), wts)
                    t_wfat[kc8 % 2] = t_watc[kc8]
                    for mcs in range(4):
                        mc = sl * 4 + mcs
                        wts = [t_watc[kc8], t_xbc[kc]]
                        if kc == 0 and sl > 0 and t_bank_rd[mcs]:
                            wts.append(t_bank_rd[mcs])
                        if kc == 0 and sl == 0:
                            if mcs == 3:
                                wts += [w for w in t_corr if w]
                            if tl == 0:
                                wts += [w for w in t_yev if w]
                                wts += [w for w in t_ps_evac if w]
                                wts += [w for w in t_awdd if w]
                        t_gmm = cT(lambda e, kc=kc, kc8=kc8, mcs=mcs, mc=mc: e.matmul(
                            PS[mcs][:], ar.bf("wat", 128, kc8 * 512 + mcs * 128),
                            ar.bf("xb1", 512, kc * 512), start=(kc == 0), stop=(kc == 15)),
                            [w for w in wts if w])
                    t_watr[kc8] = t_gmm
            for mcs in range(4):
                mc = sl * 4 + mcs
                g1s = ar.f32("g1", 512, (mc % 2) * 512)
                t_t = cA(lambda e, g=g1s, mcs=mcs: e.activation(g, PS[mcs][:], AF.Tanh, scale=0.5),
                         [t_gmm] + ([t_g1[mc % 2]] if t_g1[mc % 2] else [])
                         + (guard if mc < 2 else []))
                cV(lambda e, g=g1s: e.tensor_scalar(g, g, 0.5, 0.5, op0=ALU.mult, op1=ALU.add),
                   [t_t, t_cmby])
                t_g1[mc % 2] = cV(lambda e, mc=mc, g=g1s, rby=rby: e.tensor_tensor(
                    ar.bf("gp", 512, mc * 512), g, rby, op=ALU.mult),
                   (guard if mc < 2 else []))
                gp_done[mc] = t_g1[mc % 2]
                t_bank_rd[mcs] = t_t
        t_ag = [None] * 16
        for pr in range(16):
            t_ag[pr] = cV(lambda e, pr=pr: e.scalar_tensor_tensor(
                ar.bf("ag", 512, pr * 512), ar.bf("yf", 512, pr * 512),
                ar.f32("normw", 1, pr), ar.bf("gp", 512, pr * 512),
                op0=ALU.mult, op1=ALU.mult),
                [gp_done[pr], t_yfw[pr]] + (guard if pr == 0 else []))
        z1_done = [None] * 16
        for sl in range(4):
            for kh in range(2):
                for kc8 in range(8):
                    kc = kh * 8 + kc8
                    wsl = ar.f32("wat", 512, 2048 + (kc8 % 2) * 512)
                    td = dma("sync", wsl, outT.ap()[kc * 128:(kc + 1) * 128, sl * 512:(sl + 1) * 512],
                             waits=[w for w in (t_wfat[kc8 % 2],) if w], sem=f"dW{kc8 % 2}")
                    t_watc[kc8] = cA(lambda e, kc8=kc8, w=wsl: e.copy(
                        ar.bf("wat", 512, kc8 * 512), w),
                        [td] + ([t_watr[kc8]] if t_watr[kc8] else []))
                    t_wfat[kc8 % 2] = t_watc[kc8]
                    for mcs in range(4):
                        mc = sl * 4 + mcs
                        wts = [t_watc[kc8], t_ag[kc]]
                        if kc == 0 and t_bank_rd[mcs]:
                            wts.append(t_bank_rd[mcs])
                        t_omm = cT(lambda e, kc=kc, kc8=kc8, mcs=mcs, mc=mc: e.matmul(
                            PS[mcs][:], ar.bf("wat", 128, kc8 * 512 + mcs * 128),
                            ar.bf("ag", 512, kc * 512), start=(kc == 0), stop=(kc == 15)),
                            [w for w in wts if w])
                    t_watr[kc8] = t_omm
            for mcs in range(4):
                mc = sl * 4 + mcs
                z1_done[mc] = cV(lambda e, mc=mc, mcs=mcs: e.tensor_tensor(
                    ar.f32("x32", 512, mc * 512), ar.f32("x32", 512, mc * 512),
                    PS[mcs][:], op=ALU.add), [t_omm])
                t_bank_rd[mcs] = z1_done[mc]
        t_statmm = None
        for mc in range(NCH):
            sl2 = mc % 2
            t_zc = cA(lambda e, mc=mc, sl2=sl2: e.copy(
                ar.bf("hs", 512, sl2 * 1024), ar.f32("x32", 512, mc * 512)),
                [z1_done[mc]] + ([t_statmm] if mc >= 2 else ([t_psw_user] if t_psw_user else [])))
            t_zs = cP(lambda e, sl2=sl2: e.tensor_tensor(
                ar.bf("hs", 512, sl2 * 1024 + 512), ar.bf("hs", 512, sl2 * 1024),
                ar.bf("hs", 512, sl2 * 1024), op=ALU.mult), [t_zc])
            wts = [t_zs]
            if mc == 0 and t_psw_user:
                wts.append(t_psw_user)
            cT(lambda e, mc=mc, sl2=sl2: e.matmul(
                PSW[0:1, 0:512], onec, ar.bf("hs", 512, sl2 * 1024),
                start=(mc == 0), stop=(mc == NCH - 1)), wts)
            t_statmm = cT(lambda e, mc=mc, sl2=sl2: e.matmul(
                PSW[0:1, 512:1024], onec, ar.bf("hs", 512, sl2 * 1024 + 512),
                start=(mc == 0), stop=(mc == NCH - 1)))
        m_sv = ar.f32("mrb", 512, 0)[0:1, :]
        cV(lambda e: e.tensor_scalar(m_sv, PSW[0:1, 0:512], 1.0 / DM, None, op0=ALU.mult),
           [t_statmm])
        vv = rows_f32(0)
        cV(lambda e: e.tensor_scalar(vv, PSW[0:1, 512:1024], 1.0 / DM, None, op0=ALU.mult))
        t2 = rows_f32(2)
        cV(lambda e: e.tensor_tensor(t2, m_sv, m_sv, op=ALU.mult))
        t_var = cV(lambda e: e.tensor_tensor(vv, vv, t2, op=ALU.subtract))
        t_n1, rs1 = newton(vv, 1.0, EPS_LN, [t_var])
        t_cmbm, (mb1,) = bcast([m_sv], [t_n1], slot0=0)
        t_cmb1, (rb1,) = bcast([rs1], [t_cmbm], slot0=1)
        t_x1b = [None] * 16
        for mc in range(NCH):
            cV(lambda e, mc=mc, mb=mb1: e.tensor_tensor(
                ar.f32("x32", 512, mc * 512), ar.f32("x32", 512, mc * 512), mb, op=ALU.subtract),
                [t_cmb1] if mc == 0 else [])
            t_apq = cV(lambda e, mc=mc, rb=rb1: e.tensor_tensor(
                ar.f32("x32", 512, mc * 512), ar.f32("x32", 512, mc * 512), rb, op=ALU.mult))
            t_x1b[mc] = cP(lambda e, mc=mc: e.tensor_copy(
                ar.bf("xb1", 512, mc * 512), ar.f32("x32", 512, mc * 512)),
                [t_apq] + ([t_watr[15], tok("sT")] if mc == 0 else []))
        h_done = [None] * 44
        t_sil_prev = None
        for sl in range(44):
            twchunk = [None, None]
            for gu_i in range(2):
                mcw = 2 * sl + gu_i
                gsl = ar.bf("gw", 2048, gu_i * 2048)
                wts = [w for w in (t_gw[gu_i], gu_done[mcw]) if w]
                if sl == 0:
                    wts += [w for w in t_watr if w]
                    wts.append(tok("sA"))
                twchunk[gu_i] = dma(
                    "sync", gsl.rearrange("p (k m) -> p k m", m=128),
                    wgu.ap()[mcw // 4, :, :, (mcw % 4) * 128:(mcw % 4) * 128 + 128]
                    .rearrange("k p m -> p k m"), waits=wts, sem=f"dGW{gu_i}")
            for gu_i in range(2):
                ps_t = PS[gu_i]
                gsl = ar.bf("gw", 2048, gu_i * 2048)
                for kc in range(16):
                    wts = [twchunk[gu_i]]
                    if kc == 0:
                        wts.append(t_x1b[15])
                        if gu_i == 0 and t_sil_prev:
                            wts.append(t_sil_prev)
                        if gu_i == 1 and h_done[sl - 1]:
                            wts.append(h_done[sl - 1])
                        if sl == 0:
                            wts += [w for w in (t_bank_rd[0], t_bank_rd[1]) if w]
                    t_gu = cT(lambda e, kc=kc, ps_t=ps_t, gsl=gsl: e.matmul(
                        ps_t[:], gsl[:, kc * 128:kc * 128 + 128],
                        ar.bf("xb1", 512, kc * 512), start=(kc == 0), stop=(kc == 15)),
                        [w for w in wts if w])
                t_gw[gu_i] = t_gu
            hss = ar.bf("hs", 512, (sl % 2) * 1024)
            t_sil = cA(lambda e, h=hss: e.activation(h, PS[0][:], AF.Silu),
                       [t_gu] + ([t_hs[sl % 2]] if t_hs[sl % 2] else []))
            t_hs[sl % 2] = t_sil
            t_sil_prev = t_sil
            h_done[sl] = cV(lambda e, sl=sl, h=hss: e.tensor_tensor(
                hslot(sl), h, PS[1][:], op=ALU.mult),
                [t_sil] + (guard if sl < 2 else []))
        y32_done = [None] * 16
        t_stat2mm = None
        for M in range(DN_MC):
            t_dmm = None
            for half in range(2):
                slot = (M * 2 + half) % 2
                dsl = ar.bf("dnw", 2816, slot * 2816)
                wts = [w for w in (t_dnw[slot], dn_done[4 * M + 2 * half + 1]) if w]
                if M == 0 and half == 0:
                    wts += guard + [tok("sV")]
                tdn = dma("sync", dsl.rearrange("p (c m) -> p c m", m=128),
                          wdn.ap()[M, :, half * 22:(half + 1) * 22, :], waits=wts, sem=f"dDN{slot}")
                for ci in range(22):
                    c = half * 22 + ci
                    wts = [tdn, h_done[c]]
                    if c == 0:
                        if M >= 2 and y32_done[M - 2]:
                            wts.append(y32_done[M - 2])
                        if M < 2:
                            wts += [w for w in t_hs if w]
                            wts.append(h_done[43])
                    t_dmm = cT(lambda e, c=c, ci=ci, d=dsl, M=M: e.matmul(
                        PS[M % 2][:], d[:, ci * 128:ci * 128 + 128], hslot(c),
                        start=(c == 0), stop=(c == 43)), [w for w in wts if w])
                t_dnw[slot] = t_dmm
            t_ye = cA(lambda e, M=M: e.copy(y32slot(M), PS[M % 2][:]),
                      [t_dmm] + (guard if M == 0 else []))
            y32_done[M] = t_ye
            ybs = ar.bf("ybr", 512, (M % 2) * 512)
            cP(lambda e, M=M, y=ybs: e.tensor_copy(y, y32slot(M)),
               [t_ye] + ([t_ybw[M % 2]] if t_ybw[M % 2] else [])
               + (guard if M < 2 else []))
            t_yb = cP(lambda e, y=ybs: e.tensor_tensor(y, y, y, op=ALU.mult))
            wts = [t_yb]
            if M == 0 and t_psw_user:
                wts.append(t_psw_user)
            t_stat2mm = cT(lambda e, y=ybs, M=M: e.matmul(
                PSW[0:1, 0:512], onec, y, start=(M == 0), stop=(M == DN_MC - 1)), wts)
            t_ybw[M % 2] = t_stat2mm
        t_n2, rs2 = newton(PSW[0:1, 0:512], 1.0 / DM, EPS_FFN, [t_stat2mm])
        t_cmb2, (r2b,) = bcast([rs2], [t_n2])
        z2_done = [None] * 16
        for M in range(NCH):
            cV(lambda e, M=M, rb=r2b: e.tensor_tensor(y32slot(M), y32slot(M), rb, op=ALU.mult),
               [t_cmb2, y32_done[M]] if M == 0 else [y32_done[M]])
            z2_done[M] = cV(lambda e, M=M: e.tensor_tensor(
                y32slot(M), y32slot(M), ar.f32("x32", 512, M * 512), op=ALU.add))
        t_statmm = None
        for mc in range(NCH):
            sl2 = mc % 2
            t_zc = cA(lambda e, mc=mc, sl2=sl2: e.copy(
                ar.bf("hs", 512, sl2 * 1024), y32slot(mc)),
                [z2_done[mc]] + ([t_statmm] if mc >= 2 else [t_psw_user]))
            t_zs = cP(lambda e, sl2=sl2: e.tensor_tensor(
                ar.bf("hs", 512, sl2 * 1024 + 512), ar.bf("hs", 512, sl2 * 1024),
                ar.bf("hs", 512, sl2 * 1024), op=ALU.mult), [t_zc])
            wts = [t_zs]
            if mc == 0:
                wts.append(t_psw_user)
            cT(lambda e, mc=mc, sl2=sl2: e.matmul(
                PSW[0:1, 0:512], onec, ar.bf("hs", 512, sl2 * 1024),
                start=(mc == 0), stop=(mc == NCH - 1)), wts)
            t_statmm = cT(lambda e, mc=mc, sl2=sl2: e.matmul(
                PSW[0:1, 512:1024], onec, ar.bf("hs", 512, sl2 * 1024 + 512),
                start=(mc == 0), stop=(mc == NCH - 1)))
        m_sv = ar.f32("mrb", 512, 0)[0:1, :]
        cV(lambda e: e.tensor_scalar(m_sv, PSW[0:1, 0:512], 1.0 / DM, None, op0=ALU.mult),
           [t_statmm])
        vv = rows_f32(0)
        cV(lambda e: e.tensor_scalar(vv, PSW[0:1, 512:1024], 1.0 / DM, None, op0=ALU.mult))
        t2 = rows_f32(2)
        cV(lambda e: e.tensor_tensor(t2, m_sv, m_sv, op=ALU.mult))
        t_var = cV(lambda e: e.tensor_tensor(vv, vv, t2, op=ALU.subtract))
        t_n3, rs3 = newton(vv, 1.0, EPS_LN, [t_var])
        t_cmbm3, (mb3,) = bcast([m_sv], [t_n3], slot0=0)
        t_cmb3, (rb3,) = bcast([rs3], [t_cmbm3], slot0=1)
        t_last_out = None
        t_last_x32rd = None
        for M in range(NCH):
            cV(lambda e, M=M, mb=mb3: e.tensor_tensor(
                y32slot(M), y32slot(M), mb, op=ALU.subtract), [t_cmb3] if M == 0 else [])
            t_o = cV(lambda e, M=M, rb=rb3: e.tensor_tensor(
                ar.f32("x32", 512, M * 512), y32slot(M), rb, op=ALU.mult))
            t_last_x32rd = t_o
            t_last_out = dma("scalar", yo.ap()[:, M, t0:t0 + TT],
                             ar.f32("x32", 512, M * 512), waits=[t_o], sem="dOUT")
        t_prev_out = t_last_out
        t_prev_x32_rd = t_last_x32rd
        t_prev_down = t_dnw[1] or t_dnw[0]

    dbg_w = [tok(s) for s in sem_names if s.startswith("d")]
    for name in dbg:
        if name == "y":
            dma("scalar", dbg["y"].ap(), y_st.ap(), waits=dbg_w)
        elif name == "q":
            dma("scalar", dbg["q"].ap(), q_st.ap(), waits=dbg_w)
        elif name == "wgu":
            dma("scalar", dbg["wgu"].ap(), wgu.ap(), waits=dbg_w)
        elif name == "wdn":
            dma("scalar", dbg["wdn"].ap(), wdn.ap(), waits=dbg_w)

    final_waits = [tok(s) for s in sem_names if s.startswith("d")]
    p.do("vector", lambda e: e.engine_nop(), final_waits, ())

    with nc.Block() as block:
        block.sync(lambda e: p.emit("sync", e, sems))
        block.scalar(lambda e: p.emit("scalar", e, sems))
        block.vector(lambda e: p.emit("vector", e, sems))
        block.tensor(lambda e: p.emit("tensor", e, sems))
        block.gpsimd(lambda e: p.emit("gpsimd", e, sems))
    es.close()
    return nc



def _host_prep(inputs):
    x = np.asarray(inputs["L0_attn_hidden_states"])
    qkv_w = np.asarray(inputs["L0_attn_qkv_w"])
    out_w = np.asarray(inputs["L0_attn_out_w"])
    gate_w = np.asarray(inputs["L0_attn_gate_w"])
    norm_w = np.asarray(inputs["L0_attn_norm_w"])
    gupk = np.asarray(inputs["L0_ffn_gate_up_packed"])
    gusc = np.asarray(inputs["L0_ffn_gate_up_scales"])
    dnp_ = np.asarray(inputs["L0_ffn_down_packed"])
    dnsc = np.asarray(inputs["L0_ffn_down_scales"])

    hidx = np.arange(H)[:, None] * 192 + np.arange(64)[None, :]
    order = np.concatenate([(hidx + g * 64).reshape(-1) for g in range(3)])
    qkvT = np.ascontiguousarray(qkv_w[order].T)
    gateT = np.ascontiguousarray(gate_w.T)
    outT = np.ascontiguousarray(out_w.T)
    normw = np.ascontiguousarray(norm_w.reshape(NCH, 128).T)

    ridx = np.empty(2 * ID_, np.int64)
    for c in range(44):
        ridx[(2 * c) * 128:(2 * c) * 128 + 128] = np.arange(c * 128, c * 128 + 128)
        ridx[(2 * c + 1) * 128:(2 * c + 1) * 128 + 128] = np.arange(ID_ + c * 128, ID_ + c * 128 + 128)
    gupr = np.ascontiguousarray(gupk[ridx])
    guscT = np.ascontiguousarray(gusc.T[ridx])
    dnscT = np.ascontiguousarray(dnsc.T)

    onec = np.ones((128, 8), BF)
    oner = np.ones((1, 128), BF)
    ident = np.eye(128, dtype=BF)

    in_maps = []
    for c in range(8):
        b, half = c // 2, c % 2
        xh = x[b, half * T:(half + 1) * T, :]
        x_fmh = np.ascontiguousarray(xh.T.reshape(NCH, 128, T).transpose(1, 0, 2))
        consts = _build_consts(1.0 if half == 1 else 0.0)
        in_maps.append(dict(
            x_fm=x_fmh, qkvT=qkvT, gateT=gateT, outT=outT, normw=normw,
            gup=gupr, gusc=guscT, dnp=dnp_, dnsc=dnscT,
            ddT=consts["ddT"], qdm=consts["qdm"], kdm=consts["kdm"],
            bdm=consts["bdm"], qd512=consts["qd512"],
            onec=onec, oner=oner, ident=ident,
        ))
    return in_maps


def kernel(**inputs):
    in_maps = _host_prep(inputs)
    nc = build_nc()
    res = run_bass_kernel_spmd(nc, in_maps, core_ids=list(range(8)))
    out = np.empty((4, 2 * T, DM), np.float32)
    for c in range(8):
        b, half = c // 2, c % 2
        yov = np.asarray(res.results[c]["yo"])
        out[b, half * T:(half + 1) * T, :] = yov.transpose(2, 1, 0).reshape(T, DM)
    return out
